# version 36
# baseline (speedup 1.0000x reference)
import numpy as np
from contextlib import ExitStack

import bass_rust as _bass_rust
import concourse.bass as bass
import concourse.mybir as mybir
from concourse.bass_utils import run_bass_kernel_spmd

F32 = mybir.dt.float32
BF16 = mybir.dt.bfloat16
AF = mybir.ActivationFunctionType
ALU = mybir.AluOpType
AX = mybir.AxisListType

S = 2048
D = 1024
E = 8
HF = 2048
MAC = 1024
SUB = 512
NSUB = MAC // SUB
NCH = MAC // 128
DC = D // 128
MC = HF // 128
WIN = 32
H = 16
HD = 64
ATTN = True
MACRO_ORDER = (1, 0)
DBG_SKIP = set()


class Sem:
    def __init__(self, h):
        self.h = h
        self.v = 0


def build_program():
    nc = bass.Bass("TRN2", target_bir_lowering=False)

    x_d = nc.dram_tensor("x", [S, D], F32, kind="ExternalInput").ap()
    wq_d = nc.dram_tensor("wq", [D, D], F32, kind="ExternalInput").ap()
    wk_d = nc.dram_tensor("wk", [D, D], F32, kind="ExternalInput").ap()
    wkp_d = nc.dram_tensor("wkp", [D, D], F32, kind="ExternalInput").ap()
    wv_d = nc.dram_tensor("wv", [D, D], F32, kind="ExternalInput").ap()
    wo_d = nc.dram_tensor("wo", [D, D], F32, kind="ExternalInput").ap()
    gam_d = nc.dram_tensor("gam", [D], F32, kind="ExternalInput").ap()
    bet_d = nc.dram_tensor("bet", [D], F32, kind="ExternalInput").ap()
    g_d = nc.dram_tensor("lng", [D], F32, kind="ExternalInput").ap()
    b_d = nc.dram_tensor("lnb", [D], F32, kind="ExternalInput").ap()
    rw_d = nc.dram_tensor("rw", [D, E], F32, kind="ExternalInput").ap()
    rb_d = nc.dram_tensor("rb", [E], F32, kind="ExternalInput").ap()
    w1_d = nc.dram_tensor("w1", [E, D, HF], F32, kind="ExternalInput").ap()
    b1_d = nc.dram_tensor("b1", [E, HF], F32, kind="ExternalInput").ap()
    w2_d = nc.dram_tensor("w2", [E, HF, D], F32, kind="ExternalInput").ap()
    b2_d = nc.dram_tensor("b2", [E, D], F32, kind="ExternalInput").ap()
    out_d = nc.dram_tensor("out", [S, D], F32, kind="ExternalOutput").ap()

    def A(name, shape, dtype):
        return nc.alloc_sbuf_tensor(name, shape, dtype).ap()

    z_all = A("z_all", [128, DC * MAC], BF16)
    s_all = A("s_all", [128, DC * MAC], BF16)
    mix = A("mix", [128, DC * MAC], F32)
    hidw = A("hidw", [128, MC * SUB], BF16)
    w1b = A("w1b", [128, DC * HF], BF16)
    w2b = A("w2b", [128, MC * D], BF16)
    w_bc = A("w_bc", [128, MAC], BF16)
    b1_sb = A("b1_sb", [1, HF], BF16)
    b2_sb = A("b2_sb", [8, D], BF16)
    b1f = A("b1f", [128, MC], F32)
    hsc = A("hsc", [128, 2 * SUB], BF16)
    w_fmT = A("w_fmT", [8, MAC], BF16)
    xnn32_h = nc.alloc_sbuf_tensor("xnn32", [128, DC * 256], F32)
    xnn32 = xnn32_h.ap()
    ring_h = nc.alloc_sbuf_tensor("ring", [128, 4 * 1024], F32)
    ring = ring_h.ap()
    xc = [ring[:, 0:1024], ring[:, 1024:2048]]
    xn = [ring[:, 2048:3072], ring[:, 3072:4096]]

    ident = A("ident", [128, 128], F32)
    onesb = A("onesb", [1, SUB], BF16)
    onesf = A("onesf", [1, 256], F32)
    e8sel = A("e8sel", [8, 8 * 128], BF16)
    rw32 = A("rw32", [128, DC * E], F32)
    rb32 = A("rb32", [1, E], F32)
    gb4 = A("gb4", [128, 4 * DC], F32)
    stat = A("stat", [128, 32], F32)
    cbuf = A("cbuf", [128, 2], F32)
    nc.const_aps.register(F32, 1e-5, cbuf[:, 0:1])
    nc.const_aps.register(F32, 0.125, cbuf[:, 1:2])
    rt8 = A("rt8", [128, 24], F32)
    rt1 = A("rt1", [128, 8], F32)
    lgsb = A("lgsb", [8, 256], F32)
    lgtm = A("lgtm", [128, 16], F32)
    wtm = A("wtm", [128, 8], F32)

    if ATTN:
        aw32a_h = nc.alloc_sbuf_tensor("aw32a", [128, D], F32)
        aw32b_h = nc.alloc_sbuf_tensor("aw32b", [128, D], F32)
        _awa = nc.lookup_mloc(aw32a_h).addr
        _awb = nc.lookup_mloc(aw32b_h).addr
        awr = [nc.alloc_sbuf_tensor_at("awr%d" % i, [128, D], BF16,
                                       offset=off).ap()
               for i, off in enumerate((_awa, _awa + 2048,
                                        _awb, _awb + 2048))]
        dt = nc.alloc_sbuf_tensor_at("dt", [128, 2 * 512], F32,
                                     offset=nc.lookup_mloc(aw32a_h).addr).ap()
        ost = nc.alloc_sbuf_tensor_at("ost", [128, 2 * 512], F32,
                                      offset=nc.lookup_mloc(aw32b_h).addr).ap()

        xw32 = A("xw32", [128, DC * WIN], F32)
        xw32b = A("xw32b", [128, DC * WIN], BF16)
        qkvT_h = nc.alloc_sbuf_tensor("qkvT", [64, 3 * H * WIN], BF16)
        qkvT = qkvT_h.ap()
        bj = A("bj", [32, 8 * 128], BF16)
        bk = A("bk", [32, 128], BF16)
        mask = A("mask", [WIN, WIN * WIN], F32)
        rno = A("rno", [WIN, 3 * H], F32)
        pt_sb = A("pt_sb", [128, 8 * WIN], BF16)
        vrp2 = A("vrp2", [128, HD], F32)
        uvt = A("uvt", [128, 8 * HD], BF16)
        zt = A("zt", [128, (H // 2) * WIN], BF16)
        wst = A("wst", [WIN, 8], F32)
        rbase = nc.lookup_mloc(ring_h).addr
        AT = nc.alloc_sbuf_tensor_at
        sqs = AT("sqs", [WIN, D], F32, offset=rbase + 0).ap()
        u2k = AT("u2k", [64, WIN * WIN], BF16, offset=rbase + 4096).ap()
        u2k2 = AT("u2k2", [64, WIN * WIN], BF16, offset=rbase + 14336).ap()
        a_sb = AT("a_sb", [WIN, WIN * WIN], F32, offset=rbase + 6144).ap()
        v_tm = AT("v_tm", [WIN, D], BF16, offset=rbase + 10240).ap()
        bk_t = AT("bk_t", [32, 4 * 128], BF16, offset=rbase + 14336).ap()
        gt = AT("gt", [128, 2 * 512], BF16,
                offset=nc.lookup_mloc(qkvT_h).addr).ap()
        ao_tm = AT("ao_tm", [WIN, D], F32,
                   offset=nc.lookup_mloc(xnn32_h).addr).ap()
    else:
        gt = A("gt", [128, 2 * 512], BF16)
        dt = A("dt", [128, 2 * 512], F32)
        ost = A("ost", [128, 2 * 512], F32)

    def P(name, shape, dtype=F32):
        return nc.alloc_psum_tensor(name, shape, dtype).ap()

    pb1 = [P("pb1a", [128, SUB]), P("pb1b", [128, SUB])]
    pb2 = [P("pb2a", [128, SUB]), P("pb2b", [128, SUB])]
    pwb = P("pwb", [128, SUB])
    pta = P("pta", [128, SUB])
    ptb = P("ptb", [128, SUB])
    pzb = P("pzb", [128, SUB])
    ptr = [pta, ptb]

    es = ExitStack()
    sPE = Sem(es.enter_context(nc.semaphore("tPE")))
    sDV = Sem(es.enter_context(nc.semaphore("tDV")))
    sAC = Sem(es.enter_context(nc.semaphore("tAC")))
    sGP = Sem(es.enter_context(nc.semaphore("tGP")))
    sCX = Sem(es.enter_context(nc.semaphore("dCX")))
    sB2 = Sem(es.enter_context(nc.semaphore("dB2")))
    sW1 = Sem(es.enter_context(nc.semaphore("dW1")))
    sW2 = Sem(es.enter_context(nc.semaphore("dW2")))
    sXC = [Sem(es.enter_context(nc.semaphore("dXC0"))),
           Sem(es.enter_context(nc.semaphore("dXC1")))]
    sCPY = Sem(es.enter_context(nc.semaphore("dCPY")))
    sAW = [Sem(es.enter_context(nc.semaphore("dAW%d" % i))) for i in range(4)]
    sOS = [Sem(es.enter_context(nc.semaphore("dOS0"))),
           Sem(es.enter_context(nc.semaphore("dOS1")))]

    waited = {k: {} for k in ("tensor", "vector", "scalar", "gpsimd", "sync")}

    def W(eng, engname, sem, val):
        if val is None or val <= 0:
            return
        if waited[engname].get(id(sem), -1) >= val:
            return
        eng.wait_ge(sem.h, val)
        waited[engname][id(sem)] = val

    def I(ins, sem, n=1):
        sem.v += n
        ins.then_inc(sem.h, n)
        return sem.v

    st = {
        "ring_free": [None, None, None, None],
        "pt_free": [0, 0],
        "pt_free_a": [0, 0],
        "pt_free_ac": 0,
        "pzb_free": 0,
        "pwb_free": 0,
        "pb1_free": [0, 0],
        "pb2_free": [0, 0],
        "hsc_free": [0, 0],
        "hw_rdy": [0] * MC,
        "mm1_last": 0, "mm2_last": 0,
        "w1_rdy": 0, "w2_rdy": 0,
        "z_rdy": 0, "wfmT_rdy": 0, "mix_rdy": 0, "madd_done": 0,
        "wbc_rdy": 0,
        "epi_silu_done": 0,
        "s_patch": 0, "attn_ring_free": 0,
        "gt_free": [0, 0],
        "ost_free": [0, 0],
        "aw_free": [None, None, None, None],
        "u2k_free2": [0, 0], "uk_done0": 0, "uk_done1": 0, "a_free": 0,
        "consts_x": 0,
    }

    with nc.Block() as block:
        def sec(engname, fn):
            getattr(block, engname)(fn)

        def TP(t, out_ap, in_ap):
            p = in_ap.shape[0]
            return t.transpose(out_ap, in_ap, ident[0:p, 0:p])

        def _c_gp(g):
            def GS(ins):
                I(ins, sGP, 1)
                g.wait_ge(sGP.h, sGP.v)
            GS(g.memset(ident[:, :], 0.0))
            GS(g.affine_select(out=ident[:, :], in_=ident[:, :], compare_op=ALU.not_equal,
                            fill=1.0, base=0, pattern=[[-1, 128]], channel_multiplier=1))
            GS(g.memset(onesb[:, :], 1.0))
            GS(g.memset(onesf[:, :], 1.0))
            GS(g.memset(e8sel[:, :], 1.0))
            e83 = e8sel[:, :].rearrange("p (e q) -> p e q", e=8)
            GS(g.affine_select(out=e83, in_=e83, compare_op=ALU.is_equal, fill=0.0,
                            base=0, pattern=[[-1, 8], [0, 128]], channel_multiplier=1))
            if ATTN:
                GS(g.memset(mask[:, :], 0.0))
                m3 = mask[:, :].rearrange("p (j k) -> p j k", j=WIN)
                GS(g.affine_select(out=m3, in_=m3, compare_op=ALU.is_ge, fill=-1e30,
                                base=0, pattern=[[-1, WIN], [0, WIN]],
                                channel_multiplier=1))
                GS(g.affine_select(out=m3, in_=m3, compare_op=ALU.is_ge, fill=-1e30,
                                base=0, pattern=[[0, WIN], [-1, WIN]],
                                channel_multiplier=1))
                GS(g.memset(bj[:, :], 1.0))
                b3 = bj[:, :].rearrange("j (t p) -> j t p", t=8)
                GS(g.affine_select(out=b3, in_=b3, compare_op=ALU.is_ge, fill=0.0,
                                base=0, pattern=[[128, 8], [1, 128]],
                                channel_multiplier=-32))
                GS(g.affine_select(out=b3, in_=b3, compare_op=ALU.is_ge, fill=0.0,
                                base=31, pattern=[[-128, 8], [-1, 128]],
                                channel_multiplier=32))
                GS(g.memset(bk_t[:, :], 0.0))
                bkt = bk_t[:, :].rearrange("k (j p) -> k j p", j=4)
                GS(g.affine_select(out=bkt, in_=bkt, compare_op=ALU.not_equal, fill=1.0,
                                base=0, pattern=[[-32, 4], [1, 128]],
                                channel_multiplier=-1))
            GS(g.memset(cbuf[:, 0:1], 1e-5))
            GS(g.memset(cbuf[:, 1:2], 0.125))
            GS(g.memset(stat[:, :], 0.0))
        sec("gpsimd", _c_gp)
        st_gp_done = sGP.v

        if ATTN:
            def _c_dv(v):
                W(v, "vector", sGP, st_gp_done)
                ins = v.tensor_add(bk[:, :], bk_t[:, 0:128], bk_t[:, 128:256])
                I(ins, sDV, 1)
                W(v, "vector", sDV, sDV.v)
                ins = v.tensor_add(bk[:, :], bk[:, :], bk_t[:, 256:384])
                I(ins, sDV, 1)
                W(v, "vector", sDV, sDV.v)
                ins = v.tensor_add(bk[:, :], bk[:, :], bk_t[:, 384:512])
                I(ins, sDV, 1)
                st["ring_free"][3] = (sDV, sDV.v)
            sec("vector", _c_dv)

        def _c_sync(sy):
            ins = sy.dma_start(out=rb32[:, :], in_=rb_d.unsqueeze(0))
            I(ins, sCX, 16)
            for dc in range(DC):
                ins = sy.dma_start(out=rw32[:, dc * E:(dc + 1) * E],
                                   in_=rw_d[dc * 128:(dc + 1) * 128, :])
                I(ins, sCX, 16)
            for i, t in enumerate((g_d, b_d, gam_d, bet_d)):
                ins = sy.dma_start(out=gb4[:, i * DC:(i + 1) * DC],
                                   in_=t.rearrange("(dc p) -> p dc", p=128),
                                   allow_slow_non_contiguous=True)
                I(ins, sCX, 16)
        sec("sync", _c_sync)
        st["consts_x"] = sCX.v

        def _c_gpdma(g):
            ins = g.dma_start(out=b2_sb[:, :], in_=b2_d)
            I(ins, sB2, 16)
        sec("gpsimd", _c_gpdma)

        def _rf_pe(t):
            if "rfold" in DBG_SKIP:
                return
            W(t, "tensor", sCX, st["consts_x"])
            W(t, "tensor", sGP, st_gp_done)
            ins = None
            for dc in range(DC):
                ins = t.matmul(pzb[0:1, 0:8], gb4[:, DC + dc:DC + dc + 1],
                               rw32[:, dc * E:(dc + 1) * E],
                               start=(dc == 0), stop=(dc == DC - 1))
            I(ins, sPE, 1)
        sec("tensor", _rf_pe)
        rf_pe = sPE.v

        def _rf_dv(v, rf_pe=rf_pe):
            if "rfold" in DBG_SKIP:
                st["rfold_done"] = 0
                return
            W(v, "vector", sPE, rf_pe)
            for dc in range(DC):
                ins = v.tensor_scalar(out=rw32[:, dc * E:(dc + 1) * E],
                                      in0=rw32[:, dc * E:(dc + 1) * E],
                                      scalar1=gb4[:, dc:dc + 1], scalar2=None,
                                      op0=ALU.mult)
                I(ins, sDV, 1)
            ins = v.tensor_tensor(rb32[:, :], rb32[:, :], pzb[0:1, 0:8], ALU.add)
            I(ins, sDV, 1)
            st["pzb_free"] = sDV.v
            st["rfold_done"] = sDV.v
        sec("vector", _rf_dv)

        q_e0, q_e5a, q_e7 = [], [], []

        def drain_q(q, n):
            while n > 0 and q:
                q.pop(0)()
                n -= 1

        def phase1(mu, mi, defer_head=False):
            tok0 = mu * MAC

            def emit_ln(c, stg, mu=mu, tok0=tok0):
                sl = c % 2
                B = 8 * (c % 4)
                if stg == "a1":
                    def _dx(sy, c=c, sl=sl):
                        fr = st["ring_free"][sl]
                        if fr is not None:
                            W(sy, "sync", fr[0], fr[1])
                        ins = sy.dma_start(
                            out=xc[sl][:, :],
                            in_=x_d[tok0 + c * 128: tok0 + (c + 1) * 128, :])
                        I(ins, sXC[sl], 16)
                        st["xrdy%d" % c] = sXC[sl].v
                    sec("sync", _dx)

                    def _sq(s, c=c, sl=sl, B=B):
                        W(s, "scalar", sGP, st_gp_done)
                        W(s, "scalar", sXC[sl], st["xrdy%d" % c])
                        fr = st["ring_free"][2 + sl]
                        if fr is not None:
                            W(s, "scalar", fr[0], fr[1])
                        ins = s.activation(xn[sl][:, :], xc[sl][:, :],
                                           AF.Square,
                                           accum_out=stat[:, B + 1:B + 2])
                        I(ins, sAC, 1)
                        st["sqa%d" % c] = sAC.v
                    sec("scalar", _sq)
                    return

                pass2 = stg in ("a3", "a4")
                src = xn[sl] if pass2 else xc[sl]
                dst = xc[sl] if pass2 else xn[sl]
                if stg == "a3":
                    def _sq2(s, c=c, src=src, dst=dst, B=B):
                        W(s, "scalar", sDV, st["xn_done%d" % c])
                        ins = s.activation(dst[:, :], src[:, :], AF.Square,
                                           accum_out=stat[:, B + 1:B + 2])
                        I(ins, sAC, 1)
                        st["sqa%d" % c] = sAC.v
                    sec("scalar", _sq2)
                    return

                def _st(v, c=c, sl=sl, src=src, B=B, pass2=pass2):
                    W(v, "vector", sXC[sl], st["xrdy%d" % c])
                    if pass2:
                        W(v, "vector", sDV, st["xn_done%d" % c])
                    ins = v.reduce_sum(stat[:, B:B + 1], src[:, :], axis=AX.X)
                    I(ins, sDV, 1)
                    W(v, "vector", sAC, st["sqa%d" % c])
                    W(v, "vector", sDV, sDV.v)
                    ins = v.tensor_scalar(out=stat[:, B + 2:B + 3],
                                          in0=stat[:, B:B + 1],
                                          scalar1=1.0 / D, scalar2=None,
                                          op0=ALU.mult)
                    I(ins, sDV, 1)
                    W(v, "vector", sDV, sDV.v)
                    ins = v.tensor_tensor(stat[:, B + 4:B + 5],
                                          stat[:, B + 2:B + 3],
                                          stat[:, B + 2:B + 3], ALU.mult)
                    I(ins, sDV, 1)
                    W(v, "vector", sDV, sDV.v)
                    ins = v.scalar_tensor_tensor(out=stat[:, B + 3:B + 4],
                                                 in0=stat[:, B + 1:B + 2],
                                                 scalar=1.0 / D,
                                                 in1=stat[:, B + 4:B + 5],
                                                 op0=ALU.mult,
                                                 op1=ALU.subtract)
                    I(ins, sDV, 1)
                    st["var%d" % c] = sDV.v
                sec("vector", _st)

                def _rs(s, c=c, B=B):
                    W(s, "scalar", sDV, st["var%d" % c])
                    ins = s.activation(stat[:, B + 5:B + 6],
                                       stat[:, B + 3:B + 4], AF.Ln, bias=1e-5)
                    I(ins, sAC, 1)
                    W(s, "scalar", sAC, sAC.v)
                    ins = s.activation(stat[:, B + 6:B + 7],
                                       stat[:, B + 5:B + 6], AF.Exp,
                                       scale=-0.5)
                    I(ins, sAC, 1)
                    st["rstd%d" % c] = sAC.v
                sec("scalar", _rs)

                def _ap(v, c=c, src=src, dst=dst, B=B, pass2=pass2):
                    W(v, "vector", sAC, st["rstd%d" % c])
                    ins = v.tensor_tensor(stat[:, B + 7:B + 8],
                                          stat[:, B + 2:B + 3],
                                          stat[:, B + 6:B + 7], ALU.mult)
                    I(ins, sDV, 1)
                    W(v, "vector", sDV, sDV.v)
                    ins = v.tensor_scalar(
                        out=dst[:, :], in0=src[:, :],
                        scalar1=stat[:, B + 6:B + 7],
                        scalar2=stat[:, B + 7:B + 8],
                        op0=ALU.mult, op1=ALU.subtract)
                    I(ins, sDV, 1)
                    st[("xnn_done%d" if pass2 else "xn_done%d") % c] = sDV.v
                sec("vector", _ap)

            def emit_pair_b(cp, mu=mu, mi=mi):
                c0, c1 = 2 * cp, 2 * cp + 1

                for dc in range(DC):
                    bank = dc % 2

                    def _txn(t, dc=dc, bank=bank):
                        W(t, "tensor", sDV, max(st["xn_done%d" % c0],
                                                st["xn_done%d" % c1],
                                                st["pt_free"][bank]))
                        W(t, "tensor", sAC, st["pt_free_a"][bank])
                        TP(t, ptr[bank][:, 0:128],
                           xn[0][:, dc * 128:(dc + 1) * 128])
                        ins = TP(t, ptr[bank][:, 128:256],
                                 xn[1][:, dc * 128:(dc + 1) * 128])
                        I(ins, sPE, 1)
                        st["txn_done"] = sPE.v
                    sec("tensor", _txn)

                    def _exn(s, mu=mu, cp=cp, dc=dc, bank=bank):
                        W(s, "scalar", sPE, st["txn_done"])
                        if mi == 1:
                            W(s, "scalar", sAC, st["epi_silu_done"])
                        col = dc * MAC + c0 * 128
                        ins = s.copy(mix[:, col:col + 256], ptr[bank][:, 0:256])
                        I(ins, sAC, 1)
                        if ATTN and mu == 1 and cp == NCH // 2 - 1:
                            ins = s.copy(xw32[:, dc * WIN:(dc + 1) * WIN],
                                         ptr[bank][:, 224:256])
                            I(ins, sAC, 1)
                            ins = s.copy(xw32b[:, dc * WIN:(dc + 1) * WIN],
                                         ptr[bank][:, 224:256])
                            I(ins, sAC, 1)
                        st["pt_free_a"][bank] = sAC.v
                    sec("scalar", _exn)

                for dc in range(DC):
                    bank = dc % 2

                    def _txnn(t, dc=dc, bank=bank):
                        W(t, "tensor", sDV, max(st["xnn_done%d" % c0],
                                                st["xnn_done%d" % c1],
                                                st["pt_free"][bank]))
                        W(t, "tensor", sAC, st["pt_free_a"][bank])
                        TP(t, ptr[bank][:, 0:128],
                           xc[0][:, dc * 128:(dc + 1) * 128])
                        ins = TP(t, ptr[bank][:, 128:256],
                                 xc[1][:, dc * 128:(dc + 1) * 128])
                        I(ins, sPE, 1)
                        st["txnn_done"] = sPE.v
                    sec("tensor", _txnn)

                    def _exnn(v, cp=cp, dc=dc, bank=bank):
                        W(v, "vector", sPE, st["txnn_done"])
                        W(v, "vector", sCX, st["consts_x"])
                        col = dc * MAC + c0 * 128
                        ins = v.tensor_scalar(out=z_all[:, col:col + 256],
                                              in0=ptr[bank][:, 0:256],
                                              scalar1=gb4[:, dc:dc + 1],
                                              scalar2=gb4[:, DC + dc:DC + dc + 1],
                                              op0=ALU.mult, op1=ALU.add)
                        I(ins, sDV, 1)
                        ins = v.tensor_scalar(
                            out=s_all[:, col:col + 256],
                            in0=ptr[bank][:, 0:256],
                            scalar1=gb4[:, 2 * DC + dc:2 * DC + dc + 1],
                            scalar2=gb4[:, 3 * DC + dc:3 * DC + dc + 1],
                            op0=ALU.mult, op1=ALU.add)
                        I(ins, sDV, 1)
                        st["pt_free"][bank] = sDV.v
                        st["zs_done%d" % bank] = sDV.v
                    sec("vector", _exnn)

                    def _exnn2(s, dc=dc, bank=bank):
                        W(s, "scalar", sDV, st["zs_done%d" % bank])
                        ins = s.copy(xnn32[:, dc * 256:(dc + 1) * 256],
                                     ptr[bank][:, 0:256])
                        I(ins, sAC, 1)
                        st["pt_free_a"][bank] = sAC.v
                    sec("scalar", _exnn2)
                xnn32_done = sAC.v
                st["ring_free"][0] = (sPE, sPE.v)
                st["ring_free"][1] = (sPE, sPE.v)
                st["ring_free"][2] = (sPE, sPE.v)
                st["ring_free"][3] = (sPE, sPE.v)

                def _r1(t, xnn32_done=xnn32_done):
                    W(t, "tensor", sAC, xnn32_done)
                    W(t, "tensor", sDV, max(st["pzb_free"], st["rfold_done"]))
                    W(t, "tensor", sCX, st["consts_x"])
                    W(t, "tensor", sGP, st_gp_done)
                    ins = None
                    for q in (0, 1):
                        for dc in range(DC):
                            t.matmul(pzb[:, q * 8:(q + 1) * 8],
                                     xnn32[:, dc * 256 + q * 128:
                                           dc * 256 + (q + 1) * 128],
                                     rw32[:, dc * E:(dc + 1) * E],
                                     start=(dc == 0), stop=False)
                        ins = t.matmul(pzb[:, q * 8:(q + 1) * 8],
                                       onesf[0:1, 0:128],
                                       rb32[:, :], start=False, stop=True)
                    I(ins, sPE, 1)
                sec("tensor", _r1)
                lgd = sPE.v

                def _r2(v, lgd=lgd):
                    W(v, "vector", sPE, lgd)
                    ins = v.tensor_copy(lgtm[:, :], pzb[:, 0:16])
                    I(ins, sDV, 1)
                    st["pzb_free"] = sDV.v
                sec("vector", _r2)
                lgtm_d = sDV.v

                for q in (0, 1):
                    def _r5(v, q=q, lgtm_d=lgtm_d):
                        W(v, "vector", sDV, lgtm_d)
                        ins = v.reduce_max(rt1[:, 0:1], lgtm[:, q * 8:(q + 1) * 8],
                                           axis=AX.X, negate=True)
                        I(ins, sDV, 1)
                    sec("vector", _r5)
                    mxd = sDV.v

                    def _r5b(s, q=q, mxd=mxd):
                        W(s, "scalar", sDV, mxd)
                        ins = s.activation(rt8[:, 0:8], lgtm[:, q * 8:(q + 1) * 8],
                                           AF.Exp, bias=rt1[:, 0:1])
                        I(ins, sAC, 1)
                    sec("scalar", _r5b)
                    exd = sAC.v

                    def _r6(v, q=q, exd=exd):
                        W(v, "vector", sAC, exd)
                        ins = v.reduce_max(rt1[:, 3:4], rt8[:, 0:8], axis=AX.X)
                        I(ins, sDV, 1)
                        W(v, "vector", sDV, sDV.v)
                        ins = v.tensor_single_scalar(rt8[:, 8:16], rt8[:, 0:8],
                                                     rt1[:, 3:4], ALU.is_ge)
                        I(ins, sDV, 1)
                        W(v, "vector", sDV, sDV.v)
                        ins = v.scalar_tensor_tensor(out=rt8[:, 16:24],
                                                     in0=rt8[:, 8:16],
                                                     scalar=-1e30,
                                                     in1=rt8[:, 0:8],
                                                     op0=ALU.mult, op1=ALU.add)
                        I(ins, sDV, 1)
                        W(v, "vector", sDV, sDV.v)
                        ins = v.reduce_max(rt1[:, 4:5], rt8[:, 16:24], axis=AX.X)
                        I(ins, sDV, 1)
                        W(v, "vector", sDV, sDV.v)
                        ins = v.tensor_single_scalar(rt8[:, 8:16], rt8[:, 0:8],
                                                     rt1[:, 4:5], ALU.is_ge)
                        I(ins, sDV, 1)
                        ins = v.tensor_tensor(rt1[:, 5:6], rt1[:, 3:4],
                                              rt1[:, 4:5], ALU.add)
                        I(ins, sDV, 1)
                        W(v, "vector", sDV, sDV.v)
                        ins = v.tensor_tensor(rt8[:, 16:24], rt8[:, 8:16],
                                              rt8[:, 0:8], ALU.mult)
                        I(ins, sDV, 1)
                        ins = v.reciprocal(rt1[:, 6:7], rt1[:, 5:6])
                        I(ins, sDV, 1)
                        W(v, "vector", sDV, sDV.v)
                        ins = v.tensor_scalar(out=wtm[:, :], in0=rt8[:, 16:24],
                                              scalar1=rt1[:, 6:7], scalar2=None,
                                              op0=ALU.mult)
                        I(ins, sDV, 1)
                    sec("vector", _r6)
                    wd = sDV.v

                    def _r7(t, q=q, wd=wd):
                        W(t, "tensor", sDV, max(wd, st["pt_free"][1]))
                        W(t, "tensor", sAC, st["pt_free_a"][1])
                        ins = TP(t, ptb[0:8, q * 128:(q + 1) * 128], wtm[:, :])
                        I(ins, sPE, 1)
                    sec("tensor", _r7)
                    wtt = sPE.v

                    def _r8(v, q=q, wtt=wtt):
                        W(v, "vector", sPE, wtt)
                        ins = v.tensor_copy(
                            w_fmT[:, (c0 + q) * 128:(c0 + q + 1) * 128],
                            ptb[0:8, q * 128:(q + 1) * 128])
                        I(ins, sDV, 1)
                        if q == 1:
                            st["pt_free"][1] = sDV.v
                    sec("vector", _r8)

                if cp == 1:
                    st["z_rdy_s0"] = sDV.v
                    st["wfmT_s0"] = sDV.v
                    st["mix_s0"] = sAC.v
                if cp == NCH // 2 - 1:
                    st["z_rdy"] = sDV.v
                    st["wfmT_rdy"] = sDV.v
                    st["mix_rdy"] = sAC.v
                    if ATTN and mu == 1:
                        st["xw_rdy"] = sAC.v

            def stage_th(ca, cb, stg):
                def f():
                    emit_ln(ca, stg)
                    emit_ln(cb, stg)
                return f

            STAGES = ("a1", "a2", "a3", "a4")
            if not defer_head:
                for stg in STAGES:
                    stage_th(0, 1, stg)()
                emit_pair_b(0)
                for stg in STAGES:
                    stage_th(2, 3, stg)()
                emit_pair_b(1)
            else:
                for stg in STAGES:
                    q_e5a.append(stage_th(0, 1, stg))
                q_e7.append(lambda: emit_pair_b(0))
                for stg in STAGES:
                    q_e7.append(stage_th(2, 3, stg))
                q_e7.append(lambda: emit_pair_b(1))

            lst = []
            for cp in (2, 3):
                for stg in STAGES:
                    lst.append(stage_th(2 * cp, 2 * cp + 1, stg))
                lst.append(lambda cp=cp: emit_pair_b(cp))
            q_e0.append(lst)

        def emit_b2(mu, s_i, h_lo=0, h_hi=DC):
            for h in range(h_lo, h_hi):
                def _b2m(t, s_i=s_i, h=h):
                    W(t, "tensor", sDV,
                      max(st["wfmT_s0"] if s_i == 0 else st["wfmT_rdy"],
                          st["pzb_free"]))
                    W(t, "tensor", sB2, 16)
                    ins = t.matmul(pzb[:, 0:SUB],
                                   b2_sb[:, h * 128:(h + 1) * 128],
                                   w_fmT[:, s_i * SUB:(s_i + 1) * SUB],
                                   start=True, stop=True)
                    I(ins, sPE, 1)
                sec("tensor", _b2m)
                b2md = sPE.v

                def _b2a(v, s_i=s_i, h=h, b2md=b2md):
                    W(v, "vector", sPE, b2md)
                    W(v, "vector", sAC,
                      st["mix_s0"] if s_i == 0 else st["mix_rdy"])
                    col = h * MAC + s_i * SUB
                    ins = v.tensor_tensor(mix[:, col:col + SUB],
                                          mix[:, col:col + SUB],
                                          pzb[:, 0:SUB], ALU.add)
                    I(ins, sDV, 1)
                    st["pzb_free"] = sDV.v
                sec("vector", _b2a)
            st["mixfin_s%d_g%d" % (s_i, 0 if h_hi <= DC // 2 else 1)] = sDV.v
            if h_hi == DC:
                st["mixfin_s%d" % s_i] = sDV.v

        def emit_w1(e_target):
            def _w1d(g):
                W(g, "gpsimd", sPE, st["mm1_last"])
                ins = g.dma_start(out=b1_sb[:, :],
                                  in_=b1_d[e_target:e_target + 1, :])
                I(ins, sW1, 16)
                st["b1_rdy"] = sW1.v
                for kc in range(DC):
                    ins = g.dma_start(out=w1b[:, kc * HF:(kc + 1) * HF],
                                in_=w1_d[e_target, kc * 128:(kc + 1) * 128, :])
                    I(ins, sW1, 16)
                st["w1_rdy"] = sW1.v
            sec("gpsimd", _w1d)

        def expert_loop(mu, mi):
            cur_e0 = q_e0.pop(0) if q_e0 else []
            for e in range(E):
                if e == E - 1:
                    if ATTN and mu == 1:
                        drain_attn(len(attn_thunks))
                    drain_q(q_e5a, len(q_e5a))
                if e == 0 and not st.get("w1_prefetched"):
                    emit_w1(0)
                st["w1_prefetched"] = False
                def _wd(g, e=e):
                    W(g, "gpsimd", sPE, st["mm2_last"])
                    for kc in range(MC):
                        ins = g.dma_start(out=w2b[:, kc * D:(kc + 1) * D],
                                    in_=w2_d[e, kc * 128:(kc + 1) * 128, :])
                        I(ins, sW2, 16)
                    st["w2_rdy"] = sW2.v
                sec("gpsimd", _wd)

                def _b1f_pe(t, e=e):
                    W(t, "tensor", sW1, st["b1_rdy"])
                    W(t, "tensor", sDV, st["pwb_free"])
                    W(t, "tensor", sGP, st_gp_done)
                    ins = None
                    for m in range(MC):
                        ins = t.matmul(pwb[:, m:m + 1],
                                       b1_sb[:, m * 128:(m + 1) * 128],
                                       onesb[:, 0:1], start=True, stop=True)
                    I(ins, sPE, 1)
                sec("tensor", _b1f_pe)
                b1fp = sPE.v

                def _b1f_dv(v, b1fp=b1fp):
                    W(v, "vector", sPE, b1fp)
                    ins = v.tensor_copy(b1f[:, :], pwb[:, 0:MC])
                    I(ins, sDV, 1)
                    st["pwb_free"] = sDV.v
                sec("vector", _b1f_dv)
                st["b1f_rdy"] = sDV.v

                def emit_wbc(half, e=e):
                    def _wb1(t, e=e, half=half):
                        W(t, "tensor", sDV,
                          max(st["wfmT_s0"] if half == 0 else st["wfmT_rdy"],
                              st["pwb_free"]))
                        W(t, "tensor", sGP, st_gp_done)
                        ins = t.matmul(pwb[:, 0:SUB],
                                       e8sel[:, e * 128:(e + 1) * 128],
                                       w_fmT[:, half * SUB:(half + 1) * SUB],
                                       start=True, stop=True)
                        I(ins, sPE, 1)
                    sec("tensor", _wb1)
                    wbp = sPE.v

                    def _wb2(v, half=half, wbp=wbp):
                        W(v, "vector", sPE, wbp)
                        ins = v.tensor_copy(w_bc[:, half * SUB:(half + 1) * SUB],
                                            pwb[:, 0:SUB])
                        I(ins, sDV, 1)
                        st["pwb_free"] = sDV.v
                        st["wbc_rdy%d" % half] = sDV.v
                    sec("vector", _wb2)

                for s_i in range(NSUB):
                    emit_wbc(s_i)
                    for m in range(MC):
                        bank = m % 2

                        def _m1(t, e=e, s_i=s_i, m=m, bank=bank):
                            W(t, "tensor", sAC, st["pb1_free"][bank])
                            W(t, "tensor", sW1, st["w1_rdy"])
                            W(t, "tensor", sDV,
                              st["z_rdy_s0"] if s_i == 0 else st["z_rdy"])
                            ins = None
                            for kc in range(DC):
                                ins = t.matmul(
                                    pb1[bank][:, :],
                                    w1b[:, kc * HF + m * 128: kc * HF + (m + 1) * 128],
                                    z_all[:, kc * MAC + s_i * SUB:
                                          kc * MAC + (s_i + 1) * SUB],
                                    start=(kc == 0), stop=(kc == DC - 1))
                            I(ins, sPE, 1)
                        sec("tensor", _m1)
                        mm1g = sPE.v
                        if m == MC - 1 and s_i == NSUB - 1:
                            st["mm1_last"] = sPE.v

                        def _ge(s, bank=bank, mm1g=mm1g):
                            W(s, "scalar", sPE, mm1g)
                            W(s, "scalar", sDV, st["hsc_free"][bank])
                            W(s, "scalar", sDV, st["b1f_rdy"])
                            ins = s.activation(hsc[:, bank * SUB:(bank + 1) * SUB],
                                               pb1[bank][:, :], AF.Gelu,
                                               bias=b1f[:, m:m + 1])
                            I(ins, sAC, 1)
                            st["pb1_free"][bank] = sAC.v
                        sec("scalar", _ge)
                        gel = sAC.v

                        def _ws(v, s_i=s_i, m=m, bank=bank, gel=gel):
                            W(v, "vector", sAC, gel)
                            W(v, "vector", sDV, st["wbc_rdy%d" % s_i])
                            ins = v.tensor_tensor(
                                hidw[:, m * SUB:(m + 1) * SUB],
                                hsc[:, bank * SUB:(bank + 1) * SUB],
                                w_bc[:, s_i * SUB:(s_i + 1) * SUB], ALU.mult)
                            I(ins, sDV, 1)
                            st["hsc_free"][bank] = sDV.v
                            st["hw_rdy"][m] = sDV.v
                        sec("vector", _ws)
                        if e == 0 and s_i == 0 and m % 2 == 0:
                            drain_q(cur_e0, 1)
                        if e in (5, 6) and s_i == 0 and m % 2 == 0 \
                                and not attn_thunks:
                            drain_q(q_e5a, 1)
                        if e == E - 1 and s_i == 1:
                            if m in (2, 6, 10, 14):
                                drain_epi(1)
                            if m in (8, 10, 12, 14):
                                drain_q(q_e7, 1)
                        if ATTN and mu == 1 and not (e == 0 and s_i == 0):
                            drain_attn(1)

                    if s_i == NSUB - 1:
                        if e < E - 1:
                            emit_w1(e + 1)
                        elif mi == 0:
                            emit_w1(0)
                            st["w1_prefetched"] = True

                    for h in range(DC):
                        bank = h % 2

                        def _m2(t, s_i=s_i, h=h, bank=bank):
                            W(t, "tensor", sDV, st["pb2_free"][bank])
                            W(t, "tensor", sW2, st["w2_rdy"])
                            W(t, "tensor", sDV, st["hw_rdy"][MC - 1])
                            ins = None
                            for k2 in range(MC):
                                ins = t.matmul(
                                    pb2[bank][:, :],
                                    w2b[:, k2 * D + h * 128: k2 * D + (h + 1) * 128],
                                    hidw[:, k2 * SUB:(k2 + 1) * SUB],
                                    start=(k2 == 0), stop=(k2 == MC - 1))
                            I(ins, sPE, 1)
                        sec("tensor", _m2)
                        mm2g = sPE.v
                        if h == DC - 1 and s_i == NSUB - 1:
                            st["mm2_last"] = sPE.v

                        def _ma(v, s_i=s_i, h=h, bank=bank, mm2g=mm2g):
                            W(v, "vector", sPE, mm2g)
                            W(v, "vector", sAC,
                              st["mix_s0"] if s_i == 0 else st["mix_rdy"])
                            col = h * MAC + s_i * SUB
                            ins = v.tensor_tensor(mix[:, col:col + SUB],
                                                  mix[:, col:col + SUB],
                                                  pb2[bank][:, :], ALU.add)
                            I(ins, sDV, 1)
                            st["pb2_free"][bank] = sDV.v
                        sec("vector", _ma)
                        if e == 0 and s_i == 0 and h in (0, 2):
                            drain_q(cur_e0, 1)
                        if e == E - 1 and s_i == 1 and h in (0, 2):
                            drain_q(q_e7, 1)
                        if ATTN and mu == 1 and not (e == 0 and s_i == 0):
                            drain_attn(1)
                        if e == E - 1 and h == DC // 2 - 1:
                            emit_b2(mu, s_i, 0, DC // 2)
                    if e == E - 1:
                        emit_b2(mu, s_i, DC // 2, DC)

            st["madd_done"] = sDV.v

        attn_thunks = []

        def _th(fn):
            attn_thunks.append(fn)

        def build_attention():
            tens = (wq_d, wk_d, wkp_d, wv_d)

            def dma_th(wi, kc):
                def f():
                    slot = (wi * DC + kc) % 4

                    def _ad(g):
                        fr = st["aw_free"][slot]
                        if fr is not None:
                            W(g, "gpsimd", fr[0], fr[1])
                        ins = g.dma_start(out=awr[slot][:, :],
                                          in_=tens[wi][kc * 128:(kc + 1) * 128, :])
                        I(ins, sAW[slot], 16)
                        st["awr%d_%d" % (wi, kc)] = sAW[slot].v
                    sec("gpsimd", _ad)
                return f

            def mm_th(wi, kc):
                def f():
                    slot = (wi * DC + kc) % 4

                    def _ap(t):
                        if wi == 0 and kc == 0:
                            W(t, "tensor", sAC, st["xw_rdy"])
                        if kc == 0:
                            W(t, "tensor", sDV, max(st["pt_free"][0],
                                                    st["pt_free"][1]))
                            W(t, "tensor", sAC, max(st["pt_free_a"][0],
                                                    st["pt_free_a"][1]))
                        W(t, "tensor", sAW[slot], st["awr%d_%d" % (wi, kc)])
                        t.matmul(pta[0:WIN, 0:SUB],
                                 xw32b[:, kc * WIN:(kc + 1) * WIN],
                                 awr[slot][:, 0:SUB],
                                 start=(kc == 0), stop=(kc == DC - 1))
                        ins = t.matmul(ptb[0:WIN, 0:SUB],
                                       xw32b[:, kc * WIN:(kc + 1) * WIN],
                                       awr[slot][:, SUB:D],
                                       start=(kc == 0), stop=(kc == DC - 1))
                        I(ins, sPE, 1)
                        st["aw_free"][slot] = (sPE, sPE.v)
                        if kc == DC - 1:
                            st["proj_done%d" % wi] = sPE.v
                    sec("tensor", _ap)
                return f

            for g0 in range(4):
                _th(dma_th(g0 // DC, g0 % DC))
            pend = []
            for wi in range(4):
                for kc in range(DC):
                    gi = wi * DC + kc
                    nxt = gi + 4
                    def f(wi=wi, kc=kc, nxt=nxt):
                        mm_th(wi, kc)()
                        if nxt < 32:
                            dma_th(nxt // DC, nxt % DC)()
                    _th(f)
                if wi == 3:
                    def f_ev(wi=wi):
                        def _ae(v):
                            W(v, "vector", sPE, st["proj_done%d" % wi])
                            ins = v.tensor_copy(v_tm[:, 0:SUB], pta[0:WIN, 0:SUB])
                            I(ins, sDV, 1)
                            ins = v.tensor_copy(v_tm[:, SUB:D], ptb[0:WIN, 0:SUB])
                            I(ins, sDV, 1)
                            st["pt_free"][0] = sDV.v
                            st["pt_free"][1] = sDV.v
                            st["vtm_rdy"] = sDV.v
                        sec("vector", _ae)
                    _th(f_ev)
                    continue

                def f_ev(wi=wi):
                    def _ae(v):
                        W(v, "vector", sPE, st["proj_done%d" % wi])
                        ins = v.tensor_copy(sqs[:, 0:SUB], pta[0:WIN, 0:SUB])
                        I(ins, sDV, 1)
                        ins = v.tensor_copy(sqs[:, SUB:D], ptb[0:WIN, 0:SUB])
                        I(ins, sDV, 1)
                        st["pt_free"][0] = sDV.v
                        st["pt_free"][1] = sDV.v
                        st["sqs_rdy"] = sDV.v
                    sec("vector", _ae)

                    def _l2a(s):
                        W(s, "scalar", sDV, st["sqs_rdy"])
                        ins = s.activation(ao_tm[:, :], sqs[:, :], AF.Square)
                        I(ins, sAC, 1)
                        st["sq_done"] = sAC.v
                    sec("scalar", _l2a)

                    def _l2b(v):
                        W(v, "vector", sAC, st["sq_done"])
                        ins = None
                        for h in range(H):
                            ins = v.reduce_sum(rno[:, wi * H + h: wi * H + h + 1],
                                               ao_tm[:, h * HD:(h + 1) * HD],
                                               axis=AX.X)
                            I(ins, sDV, 1)
                        st["ssq_done"] = sDV.v
                    sec("vector", _l2b)

                    def _l2c(s):
                        W(s, "scalar", sDV, st["ssq_done"])
                        ins = s.activation(rno[:, wi * H:(wi + 1) * H],
                                           rno[:, wi * H:(wi + 1) * H], AF.Ln)
                        I(ins, sAC, 1)
                        W(s, "scalar", sAC, sAC.v)
                        ins = s.activation(rno[:, wi * H:(wi + 1) * H],
                                           rno[:, wi * H:(wi + 1) * H], AF.Exp,
                                           scale=0.5)
                        I(ins, sAC, 1)
                        st["sqrt_done"] = sAC.v
                    sec("scalar", _l2c)

                    def _l2d(v):
                        W(v, "vector", sAC, st["sqrt_done"])
                        ins = v.tensor_scalar(out=rno[:, wi * H:(wi + 1) * H],
                                              in0=rno[:, wi * H:(wi + 1) * H],
                                              scalar1=1e-7, scalar2=None,
                                              op0=ALU.add)
                        I(ins, sDV, 1)
                        W(v, "vector", sDV, sDV.v)
                        ins = v.reciprocal(rno[:, wi * H:(wi + 1) * H],
                                           rno[:, wi * H:(wi + 1) * H])
                        I(ins, sDV, 1)
                        W(v, "vector", sDV, sDV.v)
                        for h in range(H):
                            ins = v.tensor_scalar(
                                out=sqs[:, h * HD:(h + 1) * HD],
                                in0=sqs[:, h * HD:(h + 1) * HD],
                                scalar1=rno[:, wi * H + h: wi * H + h + 1],
                                scalar2=None, op0=ALU.mult)
                            I(ins, sDV, 1)
                        st["nrm_done"] = sDV.v
                    sec("vector", _l2d)
                _th(f_ev)

                for h0 in range(0, H, 2):
                    def f_tq(wi=wi, h0=h0):
                        for h in (h0, h0 + 1):
                            bank = h % 2

                            def _tq(t, h=h, bank=bank):
                                W(t, "tensor", sDV, max(st["nrm_done"],
                                                        st["pt_free"][bank]))
                                ins = TP(t, ptr[bank][0:HD, 0:WIN],
                                         sqs[:, h * HD:(h + 1) * HD])
                                I(ins, sPE, 1)
                                st["tq_done"] = sPE.v
                            sec("tensor", _tq)

                            def _eq(v, h=h, bank=bank):
                                W(v, "vector", sPE, st["tq_done"])
                                col = (wi * H + h) * WIN
                                ins = v.tensor_copy(qkvT[:, col:col + WIN],
                                                    ptr[bank][0:HD, 0:WIN])
                                I(ins, sDV, 1)
                                st["pt_free"][bank] = sDV.v
                            sec("vector", _eq)
                    _th(f_tq)
            def f_qr():
                st["qkvT_rdy"] = sDV.v
            _th(f_qr)

            def mk_uk(h):
                ub = u2k if h % 2 == 0 else u2k2
                def f():
                    def _uk(v):
                        W(v, "vector", sDV, max(st["qkvT_rdy"],
                                                st["u2k_free2"][h % 2]))
                        kcol = (1 * H + h) * WIN
                        pcol = (2 * H + h) * WIN
                        kap = qkvT[:, kcol:kcol + WIN].unsqueeze(2) \
                            .broadcast_to([64, WIN, WIN])
                        pap = qkvT[:, pcol:pcol + WIN].unsqueeze(1) \
                            .broadcast_to([64, WIN, WIN])
                        u3 = ub[:, :].rearrange("p (j k) -> p j k", j=WIN)
                        ins = v.tensor_tensor(u3, kap, pap, ALU.mult)
                        I(ins, sDV, 1)
                        st["uk_done%d" % (h % 2)] = sDV.v
                    sec("vector", _uk)
                return f

            _th(mk_uk(0))
            _th(mk_uk(1))
            for h in range(H):
                def f_att(h=h):
                    ub = u2k if h % 2 == 0 else u2k2

                    def _am(t):
                        W(t, "tensor", sDV, max(st["uk_done%d" % (h % 2)],
                                                st["pt_free"][0],
                                                st["pt_free"][1], st["a_free"]))
                        W(t, "tensor", sAC, max(st["pt_free_a"][0],
                                                st["pt_free_a"][1]))
                        qcol = h * WIN
                        t.matmul(pta[0:WIN, 0:SUB], qkvT[:, qcol:qcol + WIN],
                                 ub[:, 0:SUB], start=True, stop=True)
                        ins = t.matmul(ptb[0:WIN, 0:SUB],
                                       qkvT[:, qcol:qcol + WIN],
                                       ub[:, SUB:D], start=True, stop=True)
                        I(ins, sPE, 1)
                        st["am_done"] = sPE.v
                    sec("tensor", _am)

                    def _aev(v):
                        W(v, "vector", sPE, st["am_done"])
                        W(v, "vector", sGP, st_gp_done)
                        ins = v.tensor_tensor(a_sb[:, 0:SUB], pta[0:WIN, 0:SUB],
                                              mask[:, 0:SUB], ALU.add)
                        I(ins, sDV, 1)
                        ins = v.tensor_tensor(a_sb[:, SUB:D], ptb[0:WIN, 0:SUB],
                                              mask[:, SUB:D], ALU.add)
                        I(ins, sDV, 1)
                        st["pt_free"][0] = sDV.v
                        st["pt_free"][1] = sDV.v
                        st["u2k_free2"][h % 2] = sDV.v
                        W(v, "vector", sDV, sDV.v)
                        ins = v.reduce_max(wst[:, 1:2], a_sb[:, :], axis=AX.X)
                        I(ins, sDV, 1)
                        W(v, "vector", sDV, sDV.v)
                        ins = v.tensor_scalar(out=wst[:, 2:3], in0=wst[:, 1:2],
                                              scalar1=-0.125, scalar2=None,
                                              op0=ALU.mult)
                        I(ins, sDV, 1)
                        st["amask_done"] = sDV.v
                    sec("vector", _aev)

                    def _aex(s):
                        W(s, "scalar", sDV, st["amask_done"])
                        ins = s.activation(a_sb[:, :], a_sb[:, :], AF.Exp,
                                           bias=wst[:, 2:3], scale=0.125,
                                           accum_out=wst[:, 3:4])
                        I(ins, sAC, 1)
                        st["aexp_done"] = sAC.v
                    sec("scalar", _aex)

                    def _anr(v):
                        W(v, "vector", sAC, st["aexp_done"])
                        ins = v.reciprocal(wst[:, 4:5], wst[:, 3:4])
                        I(ins, sDV, 1)
                        W(v, "vector", sDV, sDV.v)
                        ins = v.tensor_scalar(out=a_sb[:, :], in0=a_sb[:, :],
                                              scalar1=wst[:, 4:5], scalar2=None,
                                              op0=ALU.mult)
                        I(ins, sDV, 1)
                        st["p_done"] = sDV.v
                    sec("vector", _anr)
                _th(f_att)
                if h + 2 < H:
                    _th(mk_uk(h + 2))

                for tau0 in range(0, 8, 2):
                    def f_tp(h=h, tau0=tau0):
                        for tau in (tau0, tau0 + 1):
                            bank = tau % 2

                            def _tp(t, tau=tau, bank=bank):
                                W(t, "tensor", sDV, max(st["p_done"],
                                                        st["pt_free"][bank]))
                                W(t, "tensor", sAC, st["pt_free_a"][bank])
                                ins = TP(t, ptr[bank][0:128, 0:WIN],
                                         a_sb[:, tau * 128:(tau + 1) * 128])
                                I(ins, sPE, 1)
                                st["tp_done"] = sPE.v
                            sec("tensor", _tp)

                            def _te(v, tau=tau, bank=bank):
                                W(v, "vector", sPE, st["tp_done"])
                                ins = v.tensor_copy(
                                    pt_sb[:, tau * WIN:(tau + 1) * WIN],
                                    ptr[bank][0:128, 0:WIN])
                                I(ins, sDV, 1)
                                st["pt_free"][bank] = sDV.v
                                if tau == 7:
                                    st["a_free"] = sDV.v
                            sec("vector", _te)
                    _th(f_tp)

                def f_v2(h=h):
                    def _v2(t):
                        W(t, "tensor", sDV, max(st["vtm_rdy"],
                                                st["pt_free"][0]))
                        W(t, "tensor", sAC, st["pt_free_a"][0])
                        ins = t.matmul(pta[:, 0:HD], bk[:, :],
                                       v_tm[:, h * HD:(h + 1) * HD],
                                       start=True, stop=True)
                        I(ins, sPE, 1)
                        st["v2_done"] = sPE.v
                    sec("tensor", _v2)

                    def _v2e(v):
                        W(v, "vector", sPE, st["v2_done"])
                        W(v, "vector", sDV, sDV.v)
                        ins = v.tensor_copy(vrp2[:, :], pta[:, 0:HD])
                        I(ins, sDV, 1)
                        st["pt_free"][0] = sDV.v
                        st["v2e_done"] = sDV.v
                    sec("vector", _v2e)
                _th(f_v2)

                for tau in range(8):
                    def f_uv(h=h, tau=tau):
                        bank = tau % 2

                        def _u1(t):
                            W(t, "tensor", sDV, max(st["v2e_done"],
                                                    st["pt_free"][bank]))
                            W(t, "tensor", sAC, st["pt_free_a"][bank])
                            ins = t.matmul(ptr[bank][:, 0:HD],
                                           bj[:, tau * 128:(tau + 1) * 128],
                                           v_tm[:, h * HD:(h + 1) * HD],
                                           start=True, stop=True)
                            I(ins, sPE, 1)
                            st["u1_done"] = sPE.v
                        sec("tensor", _u1)

                        def _u4(v):
                            W(v, "vector", sPE, st["u1_done"])
                            ins = v.tensor_tensor(uvt[:, tau * HD:(tau + 1) * HD],
                                                  vrp2[:, :], ptr[bank][:, 0:HD],
                                                  ALU.mult)
                            I(ins, sDV, 1)
                            st["pt_free"][bank] = sDV.v
                            if tau == 7:
                                st["uv_done"] = sDV.v
                        sec("vector", _u4)
                    _th(f_uv)

                def f_z(h=h):
                    oc = h // 2
                    base = (h % 2) * HD

                    def _zm(t):
                        W(t, "tensor", sDV, max(st["uv_done"], st["pzb_free"]))
                        ins = None
                        for tau in range(8):
                            ins = t.matmul(pzb[base:base + HD, 0:WIN],
                                           uvt[:, tau * HD:(tau + 1) * HD],
                                           pt_sb[:, tau * WIN:(tau + 1) * WIN],
                                           start=(tau == 0), stop=(tau == 7),
                                           tile_position=(0, base) if base else None)
                        I(ins, sPE, 1)
                        st["zm_done"] = sPE.v
                    sec("tensor", _zm)

                    def _ze(v):
                        W(v, "vector", sPE, st["zm_done"])
                        ins = v.tensor_copy(
                            zt[base:base + HD, oc * WIN:(oc + 1) * WIN],
                            pzb[base:base + HD, 0:WIN])
                        I(ins, sDV, 1)
                        st["pzb_free"] = sDV.v
                    sec("vector", _ze)
                _th(f_z)

            def wo_dma_th(kc):
                def f():
                    slot = kc % 4

                    def _od(g):
                        fr = st["aw_free"][slot]
                        if fr is not None:
                            W(g, "gpsimd", fr[0], fr[1])
                        ins = g.dma_start(out=awr[slot][:, :],
                                          in_=wo_d[kc * 128:(kc + 1) * 128, :])
                        I(ins, sAW[slot], 16)
                        st["wor%d" % kc] = sAW[slot].v
                    sec("gpsimd", _od)
                return f

            for k0 in range(4):
                _th(wo_dma_th(k0))
            for kc in range(DC):
                def f_om(kc=kc):
                    slot = kc % 4

                    def _om(t):
                        if kc == 0:
                            W(t, "tensor", sDV, max(st["pzb_free"],
                                                    st["pt_free"][0],
                                                    st["pt_free"][1]))
                            W(t, "tensor", sAC, max(st["pt_free_a"][0],
                                                    st["pt_free_a"][1]))
                        W(t, "tensor", sAW[slot], st["wor%d" % kc])
                        t.matmul(pta[0:WIN, 0:SUB], zt[:, kc * WIN:(kc + 1) * WIN],
                                 awr[slot][:, 0:SUB],
                                 start=(kc == 0), stop=(kc == DC - 1))
                        ins = t.matmul(ptb[0:WIN, 0:SUB],
                                       zt[:, kc * WIN:(kc + 1) * WIN],
                                       awr[slot][:, SUB:D],
                                       start=(kc == 0), stop=(kc == DC - 1))
                        I(ins, sPE, 1)
                        st["aw_free"][slot] = (sPE, sPE.v)
                        st["zo_done"] = sPE.v
                    sec("tensor", _om)
                    if kc + 4 < DC:
                        wo_dma_th(kc + 4)()
                _th(f_om)

            def f_zoe():
                def _zoe(v):
                    W(v, "vector", sPE, st["zo_done"])
                    ins = v.tensor_copy(ao_tm[:, 0:SUB], pta[0:WIN, 0:SUB])
                    I(ins, sDV, 1)
                    ins = v.tensor_copy(ao_tm[:, SUB:D], ptb[0:WIN, 0:SUB])
                    I(ins, sDV, 1)
                    st["pt_free"][0] = sDV.v
                    st["pt_free"][1] = sDV.v
                    st["zoe_done"] = sDV.v
                sec("vector", _zoe)
            _th(f_zoe)

            for dc0 in range(0, DC, 2):
                def f_rx(dc0=dc0):
                    for dc in (dc0, dc0 + 1):
                        bank = dc % 2

                        def _rx(t, dc=dc, bank=bank):
                            W(t, "tensor", sDV, max(st["zoe_done"],
                                                    st["pt_free"][bank]))
                            W(t, "tensor", sAC, st["pt_free_a"][bank])
                            ins = TP(t, ptr[bank][0:WIN, 0:128],
                                     xw32[:, dc * WIN:(dc + 1) * WIN])
                            I(ins, sPE, 1)
                            st["rx_done"] = sPE.v
                        sec("tensor", _rx)

                        def _ra(v, dc=dc, bank=bank):
                            W(v, "vector", sPE, st["rx_done"])
                            ins = v.tensor_tensor(
                                ao_tm[:, dc * 128:(dc + 1) * 128],
                                ao_tm[:, dc * 128:(dc + 1) * 128],
                                ptr[bank][0:WIN, 0:128], ALU.add)
                            I(ins, sDV, 1)
                            st["pt_free"][bank] = sDV.v
                            st["ao_done"] = sDV.v
                        sec("vector", _ra)
                _th(f_rx)

            def f_ln():
                def _w0(v):
                    W(v, "vector", sDV, st["ao_done"])
                    ins = v.reduce_sum(wst[:, 0:1], ao_tm[:, :], axis=AX.X)
                    I(ins, sDV, 1)
                    st["w0_done"] = sDV.v
                sec("vector", _w0)

                def _w1a(s):
                    W(s, "scalar", sDV, st["w0_done"])
                    ins = s.activation(sqs[:, :], ao_tm[:, :], AF.Square,
                                       accum_out=wst[:, 1:2])
                    I(ins, sAC, 1)
                    st["w1_done"] = sAC.v
                sec("scalar", _w1a)

                def _w2a(v):
                    W(v, "vector", sAC, st["w1_done"])
                    ins = v.tensor_scalar(out=wst[:, 2:3], in0=wst[:, 0:1],
                                          scalar1=1.0 / D, scalar2=None,
                                          op0=ALU.mult)
                    I(ins, sDV, 1)
                    ins = v.tensor_scalar(out=wst[:, 3:4], in0=wst[:, 1:2],
                                          scalar1=1.0 / D, scalar2=None,
                                          op0=ALU.mult)
                    I(ins, sDV, 1)
                    W(v, "vector", sDV, sDV.v)
                    ins = v.tensor_tensor(wst[:, 4:5], wst[:, 2:3], wst[:, 2:3],
                                          ALU.mult)
                    I(ins, sDV, 1)
                    W(v, "vector", sDV, sDV.v)
                    ins = v.tensor_tensor(wst[:, 4:5], wst[:, 3:4], wst[:, 4:5],
                                          ALU.subtract)
                    I(ins, sDV, 1)
                    st["w2_done"] = sDV.v
                sec("vector", _w2a)

                def _w3(s):
                    W(s, "scalar", sDV, st["w2_done"])
                    ins = s.activation(wst[:, 5:6], wst[:, 4:5], AF.Ln, bias=1e-5)
                    I(ins, sAC, 1)
                    W(s, "scalar", sAC, sAC.v)
                    ins = s.activation(wst[:, 6:7], wst[:, 5:6], AF.Exp,
                                       scale=-0.5)
                    I(ins, sAC, 1)
                    st["w3_done"] = sAC.v
                sec("scalar", _w3)

                def _w4(v):
                    W(v, "vector", sAC, st["w3_done"])
                    ins = v.tensor_tensor(wst[:, 7:8], wst[:, 2:3], wst[:, 6:7],
                                          ALU.mult)
                    I(ins, sDV, 1)
                    W(v, "vector", sDV, sDV.v)
                    ins = v.tensor_scalar(out=ao_tm[:, :], in0=ao_tm[:, :],
                                          scalar1=wst[:, 6:7],
                                          scalar2=wst[:, 7:8],
                                          op0=ALU.mult, op1=ALU.subtract)
                    I(ins, sDV, 1)
                    st["sw_done"] = sDV.v
                sec("vector", _w4)
            _th(f_ln)

            for dc0 in range(0, DC, 2):
                def f_pat(dc0=dc0):
                    for dc in (dc0, dc0 + 1):
                        bank = dc % 2

                        def _p1(t, dc=dc, bank=bank):
                            W(t, "tensor", sDV, max(st["sw_done"],
                                                    st["pt_free"][bank]))
                            W(t, "tensor", sAC, st["pt_free_a"][bank])
                            ins = TP(t, ptr[bank][0:128, 0:WIN],
                                     ao_tm[:, dc * 128:(dc + 1) * 128])
                            I(ins, sPE, 1)
                            st["p1_done"] = sPE.v
                        sec("tensor", _p1)

                        def _p2(v, dc=dc, bank=bank):
                            W(v, "vector", sPE, st["p1_done"])
                            col = dc * MAC + (MAC - WIN)
                            ins = v.tensor_scalar(
                                out=s_all[:, col:col + WIN],
                                in0=ptr[bank][0:128, 0:WIN],
                                scalar1=gb4[:, 2 * DC + dc:2 * DC + dc + 1],
                                scalar2=gb4[:, 3 * DC + dc:3 * DC + dc + 1],
                                op0=ALU.mult, op1=ALU.add)
                            I(ins, sDV, 1)
                            st["pt_free"][bank] = sDV.v
                        sec("vector", _p2)
                    if dc0 == DC - 2:
                        st["s_patch"] = sDV.v
                        st["attn_ring_free"] = sDV.v
                        for _i in range(4):
                            st["ring_free"][_i] = (sDV, sDV.v)
                _th(f_pat)

        def drain_attn(n):
            while n > 0 and attn_thunks:
                attn_thunks.pop(0)()
                n -= 1

        epi_thunks = []

        def drain_epi(n):
            while n > 0 and epi_thunks:
                epi_thunks.pop(0)()
                n -= 1

        def epilogue_mk(mu, mi):
            tok0 = mu * MAC
            def _xcp(sy, mu=mu, tok0=tok0):
                ins = sy.dma_start(out=out_d[tok0: tok0 + MAC, :],
                                   in_=x_d[tok0: tok0 + MAC, :])
                I(ins, sCPY, 16)
                for c in range(NCH):
                    st["xcopy%d" % c] = sCPY.v
            sec("sync", _xcp)
            for c in range(NCH):
                def chunk_th(c=c, mu=mu, mi=mi, tok0=tok0):
                    sl = c % 2
                    emit_chunk(mu, mi, tok0, c, sl)
                epi_thunks.append(chunk_th)

        def emit_chunk(mu, mi, tok0, c, sl):
                for dcg in range(2):
                    ti = dcg % 2

                    def _si(s, mu=mu, c=c, dcg=dcg, ti=ti):
                        W(s, "scalar", sDV,
                          st["mixfin_s%d_g%d" % (0 if c < 4 else 1, dcg)])
                        W(s, "scalar", sDV, st["gt_free"][ti])
                        msl = mix[:, :].rearrange("p (g t) -> p g t", g=DC)[
                            :, dcg * 4:(dcg + 1) * 4, c * 128:(c + 1) * 128]
                        ins = s.activation(gt[:, ti * SUB:(ti + 1) * SUB]
                                           .rearrange("p (g t) -> p g t", g=4),
                                           msl, AF.Silu)
                        I(ins, sAC, 1)
                        st["epi_silu_done"] = sAC.v
                    sec("scalar", _si)
                    sil = sAC.v

                    def _de(v, mu=mu, c=c, dcg=dcg, ti=ti, sil=sil):
                        W(v, "vector", sAC, sil)
                        if ATTN and mu == 1:
                            W(v, "vector", sDV, st["s_patch"])
                        ssl = s_all[:, :].rearrange("p (g t) -> p g t", g=DC)[
                            :, dcg * 4:(dcg + 1) * 4, c * 128:(c + 1) * 128]
                        ins = v.tensor_tensor(dt[:, ti * SUB:(ti + 1) * SUB]
                                              .rearrange("p (g t) -> p g t", g=4),
                                              gt[:, ti * SUB:(ti + 1) * SUB]
                                              .rearrange("p (g t) -> p g t", g=4),
                                              ssl, ALU.mult)
                        I(ins, sDV, 1)
                        st["gt_free"][ti] = sDV.v
                    sec("vector", _de)
                    ded = sDV.v

                    for q in range(4):
                        bank = q % 2

                        def _dt(t, ti=ti, q=q, bank=bank, ded=ded):
                            W(t, "tensor", sDV, max(ded, st["pt_free"][bank]))
                            W(t, "tensor", sAC, st["pt_free_a"][bank])
                            ins = TP(t, ptr[bank][:, 0:128],
                                     dt[:, ti * SUB + q * 128:
                                        ti * SUB + (q + 1) * 128])
                            I(ins, sPE, 1)
                        sec("tensor", _dt)
                        dtd = sPE.v

                        def _oa(v, c=c, dcg=dcg, sl=sl, ti=ti, q=q, bank=bank,
                                dtd=dtd):
                            W(v, "vector", sPE, dtd)
                            W(v, "vector", sOS[ti], st["ost_free"][ti])
                            ins = v.tensor_copy(
                                ost[:, ti * SUB + q * 128: ti * SUB + (q + 1) * 128],
                                ptr[bank][:, 0:128])
                            I(ins, sDV, 1)
                            st["pt_free"][bank] = sDV.v
                        sec("vector", _oa)
                    oad = sDV.v

                    def _os(g, mu=mu, c=c, dcg=dcg, ti=ti, oad=oad):
                        W(g, "gpsimd", sDV, oad)
                        W(g, "gpsimd", sCPY, st["xcopy%d" % c])
                        ins = g.dma_start(
                            out=out_d[tok0 + c * 128: tok0 + (c + 1) * 128,
                                      dcg * SUB:(dcg + 1) * SUB],
                            in_=ost[:, ti * SUB:(ti + 1) * SUB],
                            accum_op=ALU.add)
                        I(ins, sOS[ti], 16)
                        st["ost_free"][ti] = sOS[ti].v
                    sec("gpsimd", _os)

        phase1(MACRO_ORDER[0], 0)
        for mi, mu in enumerate(MACRO_ORDER):
            if ATTN and mu == 1:
                build_attention()
                drain_attn(4)
            epilogue_mk(mu, mi)
            if mi + 1 < len(MACRO_ORDER):
                phase1(MACRO_ORDER[mi + 1], mi + 1, defer_head=True)
            expert_loop(mu, mi)
            drain_epi(len(epi_thunks))

        def _fin(sy):
            for sem in (sOS[0], sOS[1], sXC[0], sXC[1], sCPY, *sAW,
                        sCX, sB2, sW1, sW2):
                W(sy, "sync", sem, sem.v)
        sec("sync", _fin)

    es.close()
    _bass_rust.insert_act_table_loads(nc, _act_tables() if _USE_ACT_FILTER
                                      else None)
    return nc


_USE_ACT_FILTER = True


def _act_tables():
    try:
        import json
        from pathlib import Path
        import neuronxcc
        root = Path(neuronxcc.__file__).parent / "pwp"
        cand = sorted(root.glob("pwp_bin_*/act_info.json"))
        info = json.loads(cand[0].read_bytes())
        tables = {}
        for ent in info["act_func_sets"]:
            funcs = set(ent["act"].keys())
            if ent["name"] == "exp_and_others":
                funcs.discard("exp")
            if ent["name"] == "natural_log":
                funcs.discard("ln")
            tables[ent["name"]] = {
                mybir.ActivationFunctionType.from_pwp(v) for v in funcs}
        return tables
    except Exception:
        return None


_NC_CACHE = None


def _get_nc():
    global _NC_CACHE
    if _NC_CACHE is None:
        _NC_CACHE = build_program()
    return _NC_CACHE


def kernel(**inputs):
    nc = _get_nc()
    x = np.ascontiguousarray(np.asarray(inputs["L0_attn_x"], dtype=np.float32))
    common = {
        "wq": np.ascontiguousarray(np.asarray(inputs["L0_attn_W_Q"], np.float32)),
        "wk": np.ascontiguousarray(np.asarray(inputs["L0_attn_W_K"], np.float32)),
        "wkp": np.ascontiguousarray(np.asarray(inputs["L0_attn_W_Kp"], np.float32)),
        "wv": np.ascontiguousarray(np.asarray(inputs["L0_attn_W_V"], np.float32)),
        "wo": np.ascontiguousarray(np.asarray(inputs["L0_attn_W_O"], np.float32)),
        "gam": np.ascontiguousarray(np.asarray(inputs["L0_attn_ln_gamma"], np.float32)),
        "bet": np.ascontiguousarray(np.asarray(inputs["L0_attn_ln_beta"], np.float32)),
        "lng": np.ascontiguousarray(np.asarray(inputs["L0_ffn_ln_g"], np.float32)),
        "lnb": np.ascontiguousarray(np.asarray(inputs["L0_ffn_ln_b"], np.float32)),
        "rw": np.ascontiguousarray(np.asarray(inputs["L0_ffn_rW"], np.float32)),
        "rb": np.ascontiguousarray(np.asarray(inputs["L0_ffn_rb"], np.float32)),
        "w1": np.ascontiguousarray(np.asarray(inputs["L0_ffn_W1"], np.float32)),
        "b1": np.ascontiguousarray(np.asarray(inputs["L0_ffn_b1"], np.float32)),
        "w2": np.ascontiguousarray(np.asarray(inputs["L0_ffn_W2"], np.float32)),
        "b2": np.ascontiguousarray(np.asarray(inputs["L0_ffn_b2"], np.float32)),
    }
    in_maps = [dict(common, x=np.ascontiguousarray(x[i])) for i in range(8)]
    res = run_bass_kernel_spmd(nc, in_maps, core_ids=list(range(8)))
    return np.stack([np.asarray(r["out"], np.float32) for r in res.results], axis=0)


# version 37
# speedup vs baseline: 1.0213x; 1.0213x over previous
import numpy as np
from contextlib import ExitStack

import bass_rust as _bass_rust
import concourse.bass as bass
import concourse.mybir as mybir
from concourse.bass_utils import run_bass_kernel_spmd

F32 = mybir.dt.float32
BF16 = mybir.dt.bfloat16
AF = mybir.ActivationFunctionType
ALU = mybir.AluOpType
AX = mybir.AxisListType

S = 2048
D = 1024
E = 8
HF = 2048
MAC = 1024
SUB = 512
NSUB = MAC // SUB
NCH = MAC // 128
DC = D // 128
MC = HF // 128
WIN = 32
H = 16
HD = 64
ATTN = True
MACRO_ORDER = (1, 0)
DBG_SKIP = set()


class Sem:
    def __init__(self, h):
        self.h = h
        self.v = 0


def build_program():
    nc = bass.Bass("TRN2", target_bir_lowering=False)

    x_d = nc.dram_tensor("x", [S, D], F32, kind="ExternalInput").ap()
    wq_d = nc.dram_tensor("wq", [D, D], F32, kind="ExternalInput").ap()
    wk_d = nc.dram_tensor("wk", [D, D], F32, kind="ExternalInput").ap()
    wkp_d = nc.dram_tensor("wkp", [D, D], F32, kind="ExternalInput").ap()
    wv_d = nc.dram_tensor("wv", [D, D], F32, kind="ExternalInput").ap()
    wo_d = nc.dram_tensor("wo", [D, D], F32, kind="ExternalInput").ap()
    gam_d = nc.dram_tensor("gam", [D], F32, kind="ExternalInput").ap()
    bet_d = nc.dram_tensor("bet", [D], F32, kind="ExternalInput").ap()
    g_d = nc.dram_tensor("lng", [D], F32, kind="ExternalInput").ap()
    b_d = nc.dram_tensor("lnb", [D], F32, kind="ExternalInput").ap()
    rw_d = nc.dram_tensor("rw", [D, E], F32, kind="ExternalInput").ap()
    rb_d = nc.dram_tensor("rb", [E], F32, kind="ExternalInput").ap()
    w1_d = nc.dram_tensor("w1", [E, D, HF], F32, kind="ExternalInput").ap()
    b1_d = nc.dram_tensor("b1", [E, HF], F32, kind="ExternalInput").ap()
    w2_d = nc.dram_tensor("w2", [E, HF, D], F32, kind="ExternalInput").ap()
    b2_d = nc.dram_tensor("b2", [E, D], F32, kind="ExternalInput").ap()
    out_d = nc.dram_tensor("out", [S, D], F32, kind="ExternalOutput").ap()

    def A(name, shape, dtype):
        return nc.alloc_sbuf_tensor(name, shape, dtype).ap()

    z_all = A("z_all", [128, DC * MAC], BF16)
    s_all = A("s_all", [128, DC * MAC], BF16)
    mix = A("mix", [128, DC * MAC], F32)
    hidw = A("hidw", [128, MC * SUB], BF16)
    w1b = A("w1b", [128, DC * HF], BF16)
    w2b = A("w2b", [128, MC * D], BF16)
    w_bc = A("w_bc", [128, MAC], BF16)
    b1_sb = A("b1_sb", [1, HF], BF16)
    b2_sb = A("b2_sb", [8, D], BF16)
    b1f = A("b1f", [128, MC], F32)
    hsc = A("hsc", [128, 2 * SUB], BF16)
    w_fmT = A("w_fmT", [8, MAC], BF16)
    xnn32_h = nc.alloc_sbuf_tensor("xnn32", [128, DC * 256], F32)
    xnn32 = xnn32_h.ap()
    ring_h = nc.alloc_sbuf_tensor("ring", [128, 4 * 1024], F32)
    ring = ring_h.ap()
    xc = [ring[:, 0:1024], ring[:, 1024:2048]]
    xn = [ring[:, 2048:3072], ring[:, 3072:4096]]

    ident = A("ident", [128, 128], F32)
    onesb = A("onesb", [1, SUB], BF16)
    onesf = A("onesf", [1, 256], F32)
    e8sel = A("e8sel", [8, 8 * 128], BF16)
    rw32 = A("rw32", [128, DC * E], F32)
    rb32 = A("rb32", [1, E], F32)
    gb4 = A("gb4", [128, 4 * DC], F32)
    stat = A("stat", [128, 32], F32)
    cbuf = A("cbuf", [128, 2], F32)
    nc.const_aps.register(F32, 1e-5, cbuf[:, 0:1])
    nc.const_aps.register(F32, 0.125, cbuf[:, 1:2])
    rt8 = A("rt8", [128, 24], F32)
    rt1 = A("rt1", [128, 8], F32)
    lgsb = A("lgsb", [8, 256], F32)
    lgtm = A("lgtm", [128, 16], F32)
    wtm = A("wtm", [128, 8], F32)

    if ATTN:
        aw32a_h = nc.alloc_sbuf_tensor("aw32a", [128, D], F32)
        aw32b_h = nc.alloc_sbuf_tensor("aw32b", [128, D], F32)
        _awa = nc.lookup_mloc(aw32a_h).addr
        _awb = nc.lookup_mloc(aw32b_h).addr
        awr = [nc.alloc_sbuf_tensor_at("awr%d" % i, [128, D], BF16,
                                       offset=off).ap()
               for i, off in enumerate((_awa, _awa + 2048,
                                        _awb, _awb + 2048))]
        dt = nc.alloc_sbuf_tensor_at("dt", [128, 2 * 512], F32,
                                     offset=nc.lookup_mloc(aw32a_h).addr).ap()
        ost = nc.alloc_sbuf_tensor_at("ost", [128, 2 * 512], F32,
                                      offset=nc.lookup_mloc(aw32b_h).addr).ap()

        xw32 = A("xw32", [128, DC * WIN], F32)
        xw32b = A("xw32b", [128, DC * WIN], BF16)
        qkvT_h = nc.alloc_sbuf_tensor("qkvT", [64, 3 * H * WIN], BF16)
        qkvT = qkvT_h.ap()
        bj = A("bj", [32, 8 * 128], BF16)
        bk = A("bk", [32, 128], BF16)
        mask = A("mask", [WIN, WIN * WIN], F32)
        rno = A("rno", [WIN, 3 * H], F32)
        pt_sb = A("pt_sb", [128, 8 * WIN], BF16)
        vrp2 = A("vrp2", [128, HD], F32)
        uvt = A("uvt", [128, 8 * HD], BF16)
        zt = A("zt", [128, (H // 2) * WIN], BF16)
        wst = A("wst", [WIN, 8], F32)
        rbase = nc.lookup_mloc(ring_h).addr
        AT = nc.alloc_sbuf_tensor_at
        sqs = AT("sqs", [WIN, D], F32, offset=rbase + 0).ap()
        u2k = AT("u2k", [64, WIN * WIN], BF16, offset=rbase + 4096).ap()
        u2k2 = AT("u2k2", [64, WIN * WIN], BF16, offset=rbase + 14336).ap()
        a_sb = AT("a_sb", [WIN, WIN * WIN], F32, offset=rbase + 6144).ap()
        v_tm = AT("v_tm", [WIN, D], BF16, offset=rbase + 10240).ap()
        bk_t = AT("bk_t", [32, 4 * 128], BF16, offset=rbase + 14336).ap()
        gt = AT("gt", [128, 2 * 512], BF16,
                offset=nc.lookup_mloc(qkvT_h).addr).ap()
        ao_tm = AT("ao_tm", [WIN, D], F32,
                   offset=nc.lookup_mloc(xnn32_h).addr).ap()
    else:
        gt = A("gt", [128, 2 * 512], BF16)
        dt = A("dt", [128, 2 * 512], F32)
        ost = A("ost", [128, 2 * 512], F32)

    def P(name, shape, dtype=F32):
        return nc.alloc_psum_tensor(name, shape, dtype).ap()

    pb1 = [P("pb1a", [128, SUB]), P("pb1b", [128, SUB])]
    pb2 = [P("pb2a", [128, SUB]), P("pb2b", [128, SUB])]
    pwb = P("pwb", [128, SUB])
    pta = P("pta", [128, SUB])
    ptb = P("ptb", [128, SUB])
    pzb = P("pzb", [128, SUB])
    ptr = [pta, ptb]

    es = ExitStack()
    sPE = Sem(es.enter_context(nc.semaphore("tPE")))
    sDV = Sem(es.enter_context(nc.semaphore("tDV")))
    sAC = Sem(es.enter_context(nc.semaphore("tAC")))
    sGP = Sem(es.enter_context(nc.semaphore("tGP")))
    sCX = Sem(es.enter_context(nc.semaphore("dCX")))
    sB2 = Sem(es.enter_context(nc.semaphore("dB2")))
    sW1 = Sem(es.enter_context(nc.semaphore("dW1")))
    sW2 = Sem(es.enter_context(nc.semaphore("dW2")))
    sXC = [Sem(es.enter_context(nc.semaphore("dXC0"))),
           Sem(es.enter_context(nc.semaphore("dXC1")))]
    sCPY = Sem(es.enter_context(nc.semaphore("dCPY")))
    sAW = [Sem(es.enter_context(nc.semaphore("dAW%d" % i))) for i in range(4)]
    sOS = [Sem(es.enter_context(nc.semaphore("dOS0"))),
           Sem(es.enter_context(nc.semaphore("dOS1")))]

    waited = {k: {} for k in ("tensor", "vector", "scalar", "gpsimd", "sync")}

    def W(eng, engname, sem, val):
        if val is None or val <= 0:
            return
        if waited[engname].get(id(sem), -1) >= val:
            return
        eng.wait_ge(sem.h, val)
        waited[engname][id(sem)] = val

    def I(ins, sem, n=1):
        sem.v += n
        ins.then_inc(sem.h, n)
        return sem.v

    st = {
        "ring_free": [None, None, None, None],
        "pt_free": [0, 0],
        "pt_free_a": [0, 0],
        "pt_free_ac": 0,
        "pzb_free": 0,
        "pwb_free": 0,
        "pb1_free": [0, 0],
        "pb2_free": [0, 0],
        "hsc_free": [0, 0],
        "hw_rdy": [0] * MC,
        "mm1_last": 0, "mm2_last": 0,
        "w1_rdy": 0, "w2_rdy": 0,
        "z_rdy": 0, "wfmT_rdy": 0, "mix_rdy": 0, "madd_done": 0,
        "wbc_rdy": 0,
        "epi_silu_done": 0,
        "s_patch": 0, "attn_ring_free": 0,
        "gt_free": [0, 0],
        "ost_free": [0, 0],
        "aw_free": [None, None, None, None],
        "u2k_free2": [0, 0], "uk_done0": 0, "uk_done1": 0, "a_free": 0,
        "consts_x": 0,
    }

    with nc.Block() as block:
        def sec(engname, fn):
            getattr(block, engname)(fn)

        def TP(t, out_ap, in_ap):
            p = in_ap.shape[0]
            return t.transpose(out_ap, in_ap, ident[0:p, 0:p])

        def _c_gp(g):
            def GS(ins):
                I(ins, sGP, 1)
                g.wait_ge(sGP.h, sGP.v)
            GS(g.memset(ident[:, :], 0.0))
            GS(g.affine_select(out=ident[:, :], in_=ident[:, :], compare_op=ALU.not_equal,
                            fill=1.0, base=0, pattern=[[-1, 128]], channel_multiplier=1))
            GS(g.memset(onesb[:, :], 1.0))
            GS(g.memset(onesf[:, :], 1.0))
            GS(g.memset(e8sel[:, :], 1.0))
            e83 = e8sel[:, :].rearrange("p (e q) -> p e q", e=8)
            GS(g.affine_select(out=e83, in_=e83, compare_op=ALU.is_equal, fill=0.0,
                            base=0, pattern=[[-1, 8], [0, 128]], channel_multiplier=1))
            if ATTN:
                GS(g.memset(mask[:, :], 0.0))
                m3 = mask[:, :].rearrange("p (j k) -> p j k", j=WIN)
                GS(g.affine_select(out=m3, in_=m3, compare_op=ALU.is_ge, fill=-1e30,
                                base=0, pattern=[[-1, WIN], [0, WIN]],
                                channel_multiplier=1))
                GS(g.affine_select(out=m3, in_=m3, compare_op=ALU.is_ge, fill=-1e30,
                                base=0, pattern=[[0, WIN], [-1, WIN]],
                                channel_multiplier=1))
                GS(g.memset(bj[:, :], 1.0))
                b3 = bj[:, :].rearrange("j (t p) -> j t p", t=8)
                GS(g.affine_select(out=b3, in_=b3, compare_op=ALU.is_ge, fill=0.0,
                                base=0, pattern=[[128, 8], [1, 128]],
                                channel_multiplier=-32))
                GS(g.affine_select(out=b3, in_=b3, compare_op=ALU.is_ge, fill=0.0,
                                base=31, pattern=[[-128, 8], [-1, 128]],
                                channel_multiplier=32))
                GS(g.memset(bk_t[:, :], 0.0))
                bkt = bk_t[:, :].rearrange("k (j p) -> k j p", j=4)
                GS(g.affine_select(out=bkt, in_=bkt, compare_op=ALU.not_equal, fill=1.0,
                                base=0, pattern=[[-32, 4], [1, 128]],
                                channel_multiplier=-1))
            GS(g.memset(cbuf[:, 0:1], 1e-5))
            GS(g.memset(cbuf[:, 1:2], 0.125))
            GS(g.memset(stat[:, :], 0.0))
        sec("gpsimd", _c_gp)
        st_gp_done = sGP.v

        if ATTN:
            def _c_dv(v):
                W(v, "vector", sGP, st_gp_done)
                ins = v.tensor_add(bk[:, :], bk_t[:, 0:128], bk_t[:, 128:256])
                I(ins, sDV, 1)
                W(v, "vector", sDV, sDV.v)
                ins = v.tensor_add(bk[:, :], bk[:, :], bk_t[:, 256:384])
                I(ins, sDV, 1)
                W(v, "vector", sDV, sDV.v)
                ins = v.tensor_add(bk[:, :], bk[:, :], bk_t[:, 384:512])
                I(ins, sDV, 1)
                st["ring_free"][3] = (sDV, sDV.v)
            sec("vector", _c_dv)

        def _c_sync(sy):
            ins = sy.dma_start(out=rb32[:, :], in_=rb_d.unsqueeze(0))
            I(ins, sCX, 16)
            for dc in range(DC):
                ins = sy.dma_start(out=rw32[:, dc * E:(dc + 1) * E],
                                   in_=rw_d[dc * 128:(dc + 1) * 128, :])
                I(ins, sCX, 16)
            for i, t in enumerate((g_d, b_d, gam_d, bet_d)):
                ins = sy.dma_start(out=gb4[:, i * DC:(i + 1) * DC],
                                   in_=t.rearrange("(dc p) -> p dc", p=128),
                                   allow_slow_non_contiguous=True)
                I(ins, sCX, 16)
        sec("sync", _c_sync)
        st["consts_x"] = sCX.v

        def _c_gpdma(g):
            ins = g.dma_start(out=b2_sb[:, :], in_=b2_d)
            I(ins, sB2, 16)
        sec("gpsimd", _c_gpdma)

        def _rf_pe(t):
            if "rfold" in DBG_SKIP:
                return
            W(t, "tensor", sCX, st["consts_x"])
            W(t, "tensor", sGP, st_gp_done)
            ins = None
            for dc in range(DC):
                ins = t.matmul(pzb[0:1, 0:8], gb4[:, DC + dc:DC + dc + 1],
                               rw32[:, dc * E:(dc + 1) * E],
                               start=(dc == 0), stop=(dc == DC - 1))
            I(ins, sPE, 1)
        sec("tensor", _rf_pe)
        rf_pe = sPE.v

        def _rf_dv(v, rf_pe=rf_pe):
            if "rfold" in DBG_SKIP:
                st["rfold_done"] = 0
                return
            W(v, "vector", sPE, rf_pe)
            for dc in range(DC):
                ins = v.tensor_scalar(out=rw32[:, dc * E:(dc + 1) * E],
                                      in0=rw32[:, dc * E:(dc + 1) * E],
                                      scalar1=gb4[:, dc:dc + 1], scalar2=None,
                                      op0=ALU.mult)
                I(ins, sDV, 1)
            ins = v.tensor_tensor(rb32[:, :], rb32[:, :], pzb[0:1, 0:8], ALU.add)
            I(ins, sDV, 1)
            st["pzb_free"] = sDV.v
            st["rfold_done"] = sDV.v
        sec("vector", _rf_dv)

        q_e0, q_e5a, q_e7 = [], [], []

        def drain_q(q, n):
            while n > 0 and q:
                q.pop(0)()
                n -= 1

        def phase1(mu, mi, defer_head=False):
            tok0 = mu * MAC

            def emit_ln(c, stg, mu=mu, tok0=tok0):
                sl = c % 2
                B = 8 * (c % 4)
                if stg == "a1":
                    def _dx(sy, c=c, sl=sl):
                        fr = st["ring_free"][sl]
                        if fr is not None:
                            W(sy, "sync", fr[0], fr[1])
                        ins = sy.dma_start(
                            out=xc[sl][:, :],
                            in_=x_d[tok0 + c * 128: tok0 + (c + 1) * 128, :])
                        I(ins, sXC[sl], 16)
                        st["xrdy%d" % c] = sXC[sl].v
                    sec("sync", _dx)

                    def _sq(s, c=c, sl=sl, B=B):
                        W(s, "scalar", sGP, st_gp_done)
                        W(s, "scalar", sXC[sl], st["xrdy%d" % c])
                        fr = st["ring_free"][2 + sl]
                        if fr is not None:
                            W(s, "scalar", fr[0], fr[1])
                        ins = s.activation(xn[sl][:, :], xc[sl][:, :],
                                           AF.Square,
                                           accum_out=stat[:, B + 1:B + 2])
                        I(ins, sAC, 1)
                        st["sqa%d" % c] = sAC.v
                    sec("scalar", _sq)
                    return

                pass2 = stg in ("a3", "a4")
                src = xn[sl] if pass2 else xc[sl]
                dst = xc[sl] if pass2 else xn[sl]
                if stg == "a3":
                    def _sq2(s, c=c, src=src, dst=dst, B=B):
                        W(s, "scalar", sDV, st["xn_done%d" % c])
                        ins = s.activation(dst[:, :], src[:, :], AF.Square,
                                           accum_out=stat[:, B + 1:B + 2])
                        I(ins, sAC, 1)
                        st["sqa%d" % c] = sAC.v
                    sec("scalar", _sq2)
                    return

                def _st(v, c=c, sl=sl, src=src, B=B, pass2=pass2):
                    W(v, "vector", sXC[sl], st["xrdy%d" % c])
                    if pass2:
                        W(v, "vector", sDV, st["xn_done%d" % c])
                    ins = v.reduce_sum(stat[:, B:B + 1], src[:, :], axis=AX.X)
                    I(ins, sDV, 1)
                    W(v, "vector", sAC, st["sqa%d" % c])
                    W(v, "vector", sDV, sDV.v)
                    ins = v.tensor_scalar(out=stat[:, B + 2:B + 3],
                                          in0=stat[:, B:B + 1],
                                          scalar1=1.0 / D, scalar2=None,
                                          op0=ALU.mult)
                    I(ins, sDV, 1)
                    W(v, "vector", sDV, sDV.v)
                    ins = v.tensor_tensor(stat[:, B + 4:B + 5],
                                          stat[:, B + 2:B + 3],
                                          stat[:, B + 2:B + 3], ALU.mult)
                    I(ins, sDV, 1)
                    W(v, "vector", sDV, sDV.v)
                    ins = v.scalar_tensor_tensor(out=stat[:, B + 3:B + 4],
                                                 in0=stat[:, B + 1:B + 2],
                                                 scalar=1.0 / D,
                                                 in1=stat[:, B + 4:B + 5],
                                                 op0=ALU.mult,
                                                 op1=ALU.subtract)
                    I(ins, sDV, 1)
                    st["var%d" % c] = sDV.v
                sec("vector", _st)

                def _rs(s, c=c, B=B):
                    W(s, "scalar", sDV, st["var%d" % c])
                    ins = s.activation(stat[:, B + 5:B + 6],
                                       stat[:, B + 3:B + 4], AF.Ln, bias=1e-5)
                    I(ins, sAC, 1)
                    W(s, "scalar", sAC, sAC.v)
                    ins = s.activation(stat[:, B + 6:B + 7],
                                       stat[:, B + 5:B + 6], AF.Exp,
                                       scale=-0.5)
                    I(ins, sAC, 1)
                    st["rstd%d" % c] = sAC.v
                sec("scalar", _rs)

                def _ap(v, c=c, src=src, dst=dst, B=B, pass2=pass2):
                    W(v, "vector", sAC, st["rstd%d" % c])
                    ins = v.tensor_tensor(stat[:, B + 7:B + 8],
                                          stat[:, B + 2:B + 3],
                                          stat[:, B + 6:B + 7], ALU.mult)
                    I(ins, sDV, 1)
                    W(v, "vector", sDV, sDV.v)
                    ins = v.tensor_scalar(
                        out=dst[:, :], in0=src[:, :],
                        scalar1=stat[:, B + 6:B + 7],
                        scalar2=stat[:, B + 7:B + 8],
                        op0=ALU.mult, op1=ALU.subtract)
                    I(ins, sDV, 1)
                    st[("xnn_done%d" if pass2 else "xn_done%d") % c] = sDV.v
                sec("vector", _ap)

            def emit_pair_b(cp, mu=mu, mi=mi):
                c0, c1 = 2 * cp, 2 * cp + 1

                for dc in range(DC):
                    bank = dc % 2

                    def _txn(t, dc=dc, bank=bank):
                        W(t, "tensor", sDV, max(st["xn_done%d" % c0],
                                                st["xn_done%d" % c1],
                                                st["pt_free"][bank]))
                        W(t, "tensor", sAC, st["pt_free_a"][bank])
                        TP(t, ptr[bank][:, 0:128],
                           xn[0][:, dc * 128:(dc + 1) * 128])
                        ins = TP(t, ptr[bank][:, 128:256],
                                 xn[1][:, dc * 128:(dc + 1) * 128])
                        I(ins, sPE, 1)
                        st["txn_done"] = sPE.v
                    sec("tensor", _txn)

                    def _exn(s, mu=mu, cp=cp, dc=dc, bank=bank):
                        W(s, "scalar", sPE, st["txn_done"])
                        if mi == 1:
                            W(s, "scalar", sAC, st["epi_silu_done"])
                        col = dc * MAC + c0 * 128
                        ins = s.copy(mix[:, col:col + 256], ptr[bank][:, 0:256])
                        I(ins, sAC, 1)
                        if ATTN and mu == 1 and cp == NCH // 2 - 1:
                            ins = s.copy(xw32[:, dc * WIN:(dc + 1) * WIN],
                                         ptr[bank][:, 224:256])
                            I(ins, sAC, 1)
                            ins = s.copy(xw32b[:, dc * WIN:(dc + 1) * WIN],
                                         ptr[bank][:, 224:256])
                            I(ins, sAC, 1)
                        st["pt_free_a"][bank] = sAC.v
                    sec("scalar", _exn)

                for dc in range(DC):
                    bank = dc % 2

                    def _txnn(t, dc=dc, bank=bank):
                        W(t, "tensor", sDV, max(st["xnn_done%d" % c0],
                                                st["xnn_done%d" % c1],
                                                st["pt_free"][bank]))
                        W(t, "tensor", sAC, st["pt_free_a"][bank])
                        TP(t, ptr[bank][:, 0:128],
                           xc[0][:, dc * 128:(dc + 1) * 128])
                        ins = TP(t, ptr[bank][:, 128:256],
                                 xc[1][:, dc * 128:(dc + 1) * 128])
                        I(ins, sPE, 1)
                        st["txnn_done"] = sPE.v
                    sec("tensor", _txnn)

                    def _exnn(v, cp=cp, dc=dc, bank=bank):
                        W(v, "vector", sPE, st["txnn_done"])
                        W(v, "vector", sCX, st["consts_x"])
                        col = dc * MAC + c0 * 128
                        ins = v.tensor_scalar(out=z_all[:, col:col + 256],
                                              in0=ptr[bank][:, 0:256],
                                              scalar1=gb4[:, dc:dc + 1],
                                              scalar2=gb4[:, DC + dc:DC + dc + 1],
                                              op0=ALU.mult, op1=ALU.add)
                        I(ins, sDV, 1)
                        ins = v.tensor_scalar(
                            out=s_all[:, col:col + 256],
                            in0=ptr[bank][:, 0:256],
                            scalar1=gb4[:, 2 * DC + dc:2 * DC + dc + 1],
                            scalar2=gb4[:, 3 * DC + dc:3 * DC + dc + 1],
                            op0=ALU.mult, op1=ALU.add)
                        I(ins, sDV, 1)
                        st["pt_free"][bank] = sDV.v
                        st["zs_done%d" % bank] = sDV.v
                    sec("vector", _exnn)

                    def _exnn2(s, dc=dc, bank=bank):
                        W(s, "scalar", sDV, st["zs_done%d" % bank])
                        ins = s.copy(xnn32[:, dc * 256:(dc + 1) * 256],
                                     ptr[bank][:, 0:256])
                        I(ins, sAC, 1)
                        st["pt_free_a"][bank] = sAC.v
                    sec("scalar", _exnn2)
                xnn32_done = sAC.v
                st["ring_free"][0] = (sPE, sPE.v)
                st["ring_free"][1] = (sPE, sPE.v)
                st["ring_free"][2] = (sPE, sPE.v)
                st["ring_free"][3] = (sPE, sPE.v)

                def _r1(t, xnn32_done=xnn32_done):
                    W(t, "tensor", sAC, xnn32_done)
                    W(t, "tensor", sDV, max(st["pzb_free"], st["rfold_done"]))
                    W(t, "tensor", sCX, st["consts_x"])
                    W(t, "tensor", sGP, st_gp_done)
                    ins = None
                    for q in (0, 1):
                        for dc in range(DC):
                            t.matmul(pzb[:, q * 8:(q + 1) * 8],
                                     xnn32[:, dc * 256 + q * 128:
                                           dc * 256 + (q + 1) * 128],
                                     rw32[:, dc * E:(dc + 1) * E],
                                     start=(dc == 0), stop=False)
                        ins = t.matmul(pzb[:, q * 8:(q + 1) * 8],
                                       onesf[0:1, 0:128],
                                       rb32[:, :], start=False, stop=True)
                    I(ins, sPE, 1)
                sec("tensor", _r1)
                lgd = sPE.v

                def _r2(v, lgd=lgd):
                    W(v, "vector", sPE, lgd)
                    ins = v.tensor_copy(lgtm[:, :], pzb[:, 0:16])
                    I(ins, sDV, 1)
                    st["pzb_free"] = sDV.v
                sec("vector", _r2)
                lgtm_d = sDV.v

                for q in (0, 1):
                    def _r5(v, q=q, lgtm_d=lgtm_d):
                        W(v, "vector", sDV, lgtm_d)
                        ins = v.reduce_max(rt1[:, 0:1], lgtm[:, q * 8:(q + 1) * 8],
                                           axis=AX.X, negate=True)
                        I(ins, sDV, 1)
                    sec("vector", _r5)
                    mxd = sDV.v

                    def _r5b(s, q=q, mxd=mxd):
                        W(s, "scalar", sDV, mxd)
                        ins = s.activation(rt8[:, 0:8], lgtm[:, q * 8:(q + 1) * 8],
                                           AF.Exp, bias=rt1[:, 0:1])
                        I(ins, sAC, 1)
                    sec("scalar", _r5b)
                    exd = sAC.v

                    def _r6(v, q=q, exd=exd):
                        W(v, "vector", sAC, exd)
                        ins = v.reduce_max(rt1[:, 3:4], rt8[:, 0:8], axis=AX.X)
                        I(ins, sDV, 1)
                        W(v, "vector", sDV, sDV.v)
                        ins = v.tensor_single_scalar(rt8[:, 8:16], rt8[:, 0:8],
                                                     rt1[:, 3:4], ALU.is_ge)
                        I(ins, sDV, 1)
                        W(v, "vector", sDV, sDV.v)
                        ins = v.scalar_tensor_tensor(out=rt8[:, 16:24],
                                                     in0=rt8[:, 8:16],
                                                     scalar=-1e30,
                                                     in1=rt8[:, 0:8],
                                                     op0=ALU.mult, op1=ALU.add)
                        I(ins, sDV, 1)
                        W(v, "vector", sDV, sDV.v)
                        ins = v.reduce_max(rt1[:, 4:5], rt8[:, 16:24], axis=AX.X)
                        I(ins, sDV, 1)
                        W(v, "vector", sDV, sDV.v)
                        ins = v.tensor_single_scalar(rt8[:, 8:16], rt8[:, 0:8],
                                                     rt1[:, 4:5], ALU.is_ge)
                        I(ins, sDV, 1)
                        ins = v.tensor_tensor(rt1[:, 5:6], rt1[:, 3:4],
                                              rt1[:, 4:5], ALU.add)
                        I(ins, sDV, 1)
                        W(v, "vector", sDV, sDV.v)
                        ins = v.tensor_tensor(rt8[:, 16:24], rt8[:, 8:16],
                                              rt8[:, 0:8], ALU.mult)
                        I(ins, sDV, 1)
                        ins = v.reciprocal(rt1[:, 6:7], rt1[:, 5:6])
                        I(ins, sDV, 1)
                        W(v, "vector", sDV, sDV.v)
                        ins = v.tensor_scalar(out=wtm[:, :], in0=rt8[:, 16:24],
                                              scalar1=rt1[:, 6:7], scalar2=None,
                                              op0=ALU.mult)
                        I(ins, sDV, 1)
                    sec("vector", _r6)
                    wd = sDV.v

                    def _r7(t, q=q, wd=wd):
                        W(t, "tensor", sDV, max(wd, st["pt_free"][1]))
                        W(t, "tensor", sAC, st["pt_free_a"][1])
                        ins = TP(t, ptb[0:8, q * 128:(q + 1) * 128], wtm[:, :])
                        I(ins, sPE, 1)
                    sec("tensor", _r7)
                    wtt = sPE.v

                    def _r8(v, q=q, wtt=wtt):
                        W(v, "vector", sPE, wtt)
                        ins = v.tensor_copy(
                            w_fmT[:, (c0 + q) * 128:(c0 + q + 1) * 128],
                            ptb[0:8, q * 128:(q + 1) * 128])
                        I(ins, sDV, 1)
                        if q == 1:
                            st["pt_free"][1] = sDV.v
                    sec("vector", _r8)

                if cp == 1:
                    st["z_rdy_s0"] = sDV.v
                    st["wfmT_s0"] = sDV.v
                    st["mix_s0"] = sAC.v
                if cp == NCH // 2 - 1:
                    st["z_rdy"] = sDV.v
                    st["wfmT_rdy"] = sDV.v
                    st["mix_rdy"] = sAC.v
                    if ATTN and mu == 1:
                        st["xw_rdy"] = sAC.v

            def stage_th(ca, cb, stg):
                def f():
                    emit_ln(ca, stg)
                    emit_ln(cb, stg)
                return f

            STAGES = ("a1", "a2", "a3", "a4")
            if not defer_head:
                for stg in STAGES:
                    stage_th(0, 1, stg)()
                emit_pair_b(0)
                for stg in STAGES:
                    stage_th(2, 3, stg)()
                emit_pair_b(1)
            else:
                for stg in STAGES:
                    q_e5a.append(stage_th(0, 1, stg))
                q_e7.append(lambda: emit_pair_b(0))
                for stg in STAGES:
                    q_e7.append(stage_th(2, 3, stg))
                q_e7.append(lambda: emit_pair_b(1))

            lst = []
            for cp in (2, 3):
                for stg in STAGES:
                    lst.append(stage_th(2 * cp, 2 * cp + 1, stg))
                lst.append(lambda cp=cp: emit_pair_b(cp))
            q_e0.append(lst)

        def emit_b2(mu, s_i, h_lo=0, h_hi=DC):
            for h in range(h_lo, h_hi):
                def _b2m(t, s_i=s_i, h=h):
                    W(t, "tensor", sDV,
                      max(st["wfmT_s0"] if s_i == 0 else st["wfmT_rdy"],
                          st["pzb_free"]))
                    W(t, "tensor", sB2, 16)
                    ins = t.matmul(pzb[:, 0:SUB],
                                   b2_sb[:, h * 128:(h + 1) * 128],
                                   w_fmT[:, s_i * SUB:(s_i + 1) * SUB],
                                   start=True, stop=True)
                    I(ins, sPE, 1)
                sec("tensor", _b2m)
                b2md = sPE.v

                def _b2a(v, s_i=s_i, h=h, b2md=b2md):
                    W(v, "vector", sPE, b2md)
                    W(v, "vector", sAC,
                      st["mix_s0"] if s_i == 0 else st["mix_rdy"])
                    col = h * MAC + s_i * SUB
                    ins = v.tensor_tensor(mix[:, col:col + SUB],
                                          mix[:, col:col + SUB],
                                          pzb[:, 0:SUB], ALU.add)
                    I(ins, sDV, 1)
                    st["pzb_free"] = sDV.v
                sec("vector", _b2a)
            st["mixfin_s%d_g%d" % (s_i, 0 if h_hi <= DC // 2 else 1)] = sDV.v
            if h_hi == DC:
                st["mixfin_s%d" % s_i] = sDV.v

        def emit_w1(e_target):
            def _w1d(g):
                W(g, "gpsimd", sPE, st["mm1_last"])
                ins = g.dma_start(out=b1_sb[:, :],
                                  in_=b1_d[e_target:e_target + 1, :])
                I(ins, sW1, 16)
                st["b1_rdy"] = sW1.v
                for kc in range(DC):
                    ins = g.dma_start(out=w1b[:, kc * HF:(kc + 1) * HF],
                                in_=w1_d[e_target, kc * 128:(kc + 1) * 128, :])
                    I(ins, sW1, 16)
                    st["w1_kc%d" % kc] = sW1.v
                st["w1_rdy"] = sW1.v
            sec("gpsimd", _w1d)

        def expert_loop(mu, mi):
            cur_e0 = q_e0.pop(0) if q_e0 else []
            for e in range(E):
                if e == E - 1:
                    if ATTN and mu == 1:
                        drain_attn(len(attn_thunks))
                    drain_q(q_e5a, len(q_e5a))
                if e == 0 and not st.get("w1_prefetched"):
                    emit_w1(0)
                st["w1_prefetched"] = False
                def _wd(g, e=e):
                    W(g, "gpsimd", sPE, st["mm2_last"])
                    for kc in range(MC):
                        ins = g.dma_start(out=w2b[:, kc * D:(kc + 1) * D],
                                    in_=w2_d[e, kc * 128:(kc + 1) * 128, :])
                        I(ins, sW2, 16)
                        st["w2_kc%d" % kc] = sW2.v
                    st["w2_rdy"] = sW2.v
                sec("gpsimd", _wd)

                def _b1f_pe(t, e=e):
                    W(t, "tensor", sW1, st["b1_rdy"])
                    W(t, "tensor", sDV, st["pwb_free"])
                    W(t, "tensor", sGP, st_gp_done)
                    ins = None
                    for m in range(MC):
                        ins = t.matmul(pwb[:, m:m + 1],
                                       b1_sb[:, m * 128:(m + 1) * 128],
                                       onesb[:, 0:1], start=True, stop=True)
                    I(ins, sPE, 1)
                sec("tensor", _b1f_pe)
                b1fp = sPE.v

                def _b1f_dv(v, b1fp=b1fp):
                    W(v, "vector", sPE, b1fp)
                    ins = v.tensor_copy(b1f[:, :], pwb[:, 0:MC])
                    I(ins, sDV, 1)
                    st["pwb_free"] = sDV.v
                sec("vector", _b1f_dv)
                st["b1f_rdy"] = sDV.v

                def emit_wbc(half, e=e):
                    def _wb1(t, e=e, half=half):
                        W(t, "tensor", sDV,
                          max(st["wfmT_s0"] if half == 0 else st["wfmT_rdy"],
                              st["pwb_free"]))
                        W(t, "tensor", sGP, st_gp_done)
                        ins = t.matmul(pwb[:, 0:SUB],
                                       e8sel[:, e * 128:(e + 1) * 128],
                                       w_fmT[:, half * SUB:(half + 1) * SUB],
                                       start=True, stop=True)
                        I(ins, sPE, 1)
                    sec("tensor", _wb1)
                    wbp = sPE.v

                    def _wb2(v, half=half, wbp=wbp):
                        W(v, "vector", sPE, wbp)
                        ins = v.tensor_copy(w_bc[:, half * SUB:(half + 1) * SUB],
                                            pwb[:, 0:SUB])
                        I(ins, sDV, 1)
                        st["pwb_free"] = sDV.v
                        st["wbc_rdy%d" % half] = sDV.v
                    sec("vector", _wb2)

                for s_i in range(NSUB):
                    emit_wbc(s_i)
                    for m in range(MC):
                        bank = m % 2

                        def _m1(t, e=e, s_i=s_i, m=m, bank=bank):
                            W(t, "tensor", sAC, st["pb1_free"][bank])
                            W(t, "tensor", sDV,
                              st["z_rdy_s0"] if s_i == 0 else st["z_rdy"])
                            ins = None
                            for kc in range(DC):
                                W(t, "tensor", sW1, st["w1_kc%d" % kc])
                                ins = t.matmul(
                                    pb1[bank][:, :],
                                    w1b[:, kc * HF + m * 128: kc * HF + (m + 1) * 128],
                                    z_all[:, kc * MAC + s_i * SUB:
                                          kc * MAC + (s_i + 1) * SUB],
                                    start=(kc == 0), stop=(kc == DC - 1))
                            I(ins, sPE, 1)
                        sec("tensor", _m1)
                        mm1g = sPE.v
                        if m == MC - 1 and s_i == NSUB - 1:
                            st["mm1_last"] = sPE.v

                        def _ge(s, bank=bank, mm1g=mm1g):
                            W(s, "scalar", sPE, mm1g)
                            W(s, "scalar", sDV, st["hsc_free"][bank])
                            W(s, "scalar", sDV, st["b1f_rdy"])
                            ins = s.activation(hsc[:, bank * SUB:(bank + 1) * SUB],
                                               pb1[bank][:, :], AF.Gelu,
                                               bias=b1f[:, m:m + 1])
                            I(ins, sAC, 1)
                            st["pb1_free"][bank] = sAC.v
                        sec("scalar", _ge)
                        gel = sAC.v

                        def _ws(v, s_i=s_i, m=m, bank=bank, gel=gel):
                            W(v, "vector", sAC, gel)
                            W(v, "vector", sDV, st["wbc_rdy%d" % s_i])
                            ins = v.tensor_tensor(
                                hidw[:, m * SUB:(m + 1) * SUB],
                                hsc[:, bank * SUB:(bank + 1) * SUB],
                                w_bc[:, s_i * SUB:(s_i + 1) * SUB], ALU.mult)
                            I(ins, sDV, 1)
                            st["hsc_free"][bank] = sDV.v
                            st["hw_rdy"][m] = sDV.v
                        sec("vector", _ws)
                        if e == 0 and s_i == 0 and m % 2 == 0:
                            drain_q(cur_e0, 1)
                        if e in (5, 6) and s_i == 0 and m % 2 == 0 \
                                and not attn_thunks:
                            drain_q(q_e5a, 1)
                        if e == E - 1 and s_i == 1:
                            if m in (2, 6, 10, 14):
                                drain_epi(1)
                            if m in (8, 10, 12, 14):
                                drain_q(q_e7, 1)
                        if ATTN and mu == 1 and not (e == 0 and s_i == 0):
                            drain_attn(1)

                    if s_i == NSUB - 1:
                        if e < E - 1:
                            emit_w1(e + 1)
                        elif mi == 0:
                            emit_w1(0)
                            st["w1_prefetched"] = True

                    for h in range(DC):
                        bank = h % 2

                        def _m2(t, s_i=s_i, h=h, bank=bank):
                            W(t, "tensor", sDV, st["pb2_free"][bank])
                            ins = None
                            for k2 in range(MC):
                                W(t, "tensor", sW2, st["w2_kc%d" % k2])
                                W(t, "tensor", sDV, st["hw_rdy"][k2])
                                ins = t.matmul(
                                    pb2[bank][:, :],
                                    w2b[:, k2 * D + h * 128: k2 * D + (h + 1) * 128],
                                    hidw[:, k2 * SUB:(k2 + 1) * SUB],
                                    start=(k2 == 0), stop=(k2 == MC - 1))
                            I(ins, sPE, 1)
                        sec("tensor", _m2)
                        mm2g = sPE.v
                        if h == DC - 1 and s_i == NSUB - 1:
                            st["mm2_last"] = sPE.v

                        def _ma(v, s_i=s_i, h=h, bank=bank, mm2g=mm2g):
                            W(v, "vector", sPE, mm2g)
                            W(v, "vector", sAC,
                              st["mix_s0"] if s_i == 0 else st["mix_rdy"])
                            col = h * MAC + s_i * SUB
                            ins = v.tensor_tensor(mix[:, col:col + SUB],
                                                  mix[:, col:col + SUB],
                                                  pb2[bank][:, :], ALU.add)
                            I(ins, sDV, 1)
                            st["pb2_free"][bank] = sDV.v
                        sec("vector", _ma)
                        if e == 0 and s_i == 0 and h in (0, 2):
                            drain_q(cur_e0, 1)
                        if e == E - 1 and s_i == 1 and h in (0, 2):
                            drain_q(q_e7, 1)
                        if ATTN and mu == 1 and not (e == 0 and s_i == 0):
                            drain_attn(1)
                        if e == E - 1 and h == DC // 2 - 1:
                            emit_b2(mu, s_i, 0, DC // 2)
                    if e == E - 1:
                        emit_b2(mu, s_i, DC // 2, DC)

            st["madd_done"] = sDV.v

        attn_thunks = []

        def _th(fn):
            attn_thunks.append(fn)

        def build_attention():
            tens = (wq_d, wk_d, wkp_d, wv_d)

            def dma_th(wi, kc):
                def f():
                    slot = (wi * DC + kc) % 4

                    def _ad(g):
                        fr = st["aw_free"][slot]
                        if fr is not None:
                            W(g, "gpsimd", fr[0], fr[1])
                        ins = g.dma_start(out=awr[slot][:, :],
                                          in_=tens[wi][kc * 128:(kc + 1) * 128, :])
                        I(ins, sAW[slot], 16)
                        st["awr%d_%d" % (wi, kc)] = sAW[slot].v
                    sec("gpsimd", _ad)
                return f

            def mm_th(wi, kc):
                def f():
                    slot = (wi * DC + kc) % 4

                    def _ap(t):
                        if wi == 0 and kc == 0:
                            W(t, "tensor", sAC, st["xw_rdy"])
                        if kc == 0:
                            W(t, "tensor", sDV, max(st["pt_free"][0],
                                                    st["pt_free"][1]))
                            W(t, "tensor", sAC, max(st["pt_free_a"][0],
                                                    st["pt_free_a"][1]))
                        W(t, "tensor", sAW[slot], st["awr%d_%d" % (wi, kc)])
                        t.matmul(pta[0:WIN, 0:SUB],
                                 xw32b[:, kc * WIN:(kc + 1) * WIN],
                                 awr[slot][:, 0:SUB],
                                 start=(kc == 0), stop=(kc == DC - 1))
                        ins = t.matmul(ptb[0:WIN, 0:SUB],
                                       xw32b[:, kc * WIN:(kc + 1) * WIN],
                                       awr[slot][:, SUB:D],
                                       start=(kc == 0), stop=(kc == DC - 1))
                        I(ins, sPE, 1)
                        st["aw_free"][slot] = (sPE, sPE.v)
                        if kc == DC - 1:
                            st["proj_done%d" % wi] = sPE.v
                    sec("tensor", _ap)
                return f

            for g0 in range(4):
                _th(dma_th(g0 // DC, g0 % DC))
            pend = []
            for wi in range(4):
                for kc in range(DC):
                    gi = wi * DC + kc
                    nxt = gi + 4
                    def f(wi=wi, kc=kc, nxt=nxt):
                        mm_th(wi, kc)()
                        if nxt < 32:
                            dma_th(nxt // DC, nxt % DC)()
                    _th(f)
                if wi == 3:
                    def f_ev(wi=wi):
                        def _ae(v):
                            W(v, "vector", sPE, st["proj_done%d" % wi])
                            ins = v.tensor_copy(v_tm[:, 0:SUB], pta[0:WIN, 0:SUB])
                            I(ins, sDV, 1)
                            ins = v.tensor_copy(v_tm[:, SUB:D], ptb[0:WIN, 0:SUB])
                            I(ins, sDV, 1)
                            st["pt_free"][0] = sDV.v
                            st["pt_free"][1] = sDV.v
                            st["vtm_rdy"] = sDV.v
                        sec("vector", _ae)
                    _th(f_ev)
                    continue

                def f_ev(wi=wi):
                    def _ae(v):
                        W(v, "vector", sPE, st["proj_done%d" % wi])
                        ins = v.tensor_copy(sqs[:, 0:SUB], pta[0:WIN, 0:SUB])
                        I(ins, sDV, 1)
                        ins = v.tensor_copy(sqs[:, SUB:D], ptb[0:WIN, 0:SUB])
                        I(ins, sDV, 1)
                        st["pt_free"][0] = sDV.v
                        st["pt_free"][1] = sDV.v
                        st["sqs_rdy"] = sDV.v
                    sec("vector", _ae)

                    def _l2a(s):
                        W(s, "scalar", sDV, st["sqs_rdy"])
                        ins = s.activation(ao_tm[:, :], sqs[:, :], AF.Square)
                        I(ins, sAC, 1)
                        st["sq_done"] = sAC.v
                    sec("scalar", _l2a)

                    def _l2b(v):
                        W(v, "vector", sAC, st["sq_done"])
                        ins = None
                        for h in range(H):
                            ins = v.reduce_sum(rno[:, wi * H + h: wi * H + h + 1],
                                               ao_tm[:, h * HD:(h + 1) * HD],
                                               axis=AX.X)
                            I(ins, sDV, 1)
                        st["ssq_done"] = sDV.v
                    sec("vector", _l2b)

                    def _l2c(s):
                        W(s, "scalar", sDV, st["ssq_done"])
                        ins = s.activation(rno[:, wi * H:(wi + 1) * H],
                                           rno[:, wi * H:(wi + 1) * H], AF.Ln)
                        I(ins, sAC, 1)
                        W(s, "scalar", sAC, sAC.v)
                        ins = s.activation(rno[:, wi * H:(wi + 1) * H],
                                           rno[:, wi * H:(wi + 1) * H], AF.Exp,
                                           scale=0.5)
                        I(ins, sAC, 1)
                        st["sqrt_done"] = sAC.v
                    sec("scalar", _l2c)

                    def _l2d(v):
                        W(v, "vector", sAC, st["sqrt_done"])
                        ins = v.tensor_scalar(out=rno[:, wi * H:(wi + 1) * H],
                                              in0=rno[:, wi * H:(wi + 1) * H],
                                              scalar1=1e-7, scalar2=None,
                                              op0=ALU.add)
                        I(ins, sDV, 1)
                        W(v, "vector", sDV, sDV.v)
                        ins = v.reciprocal(rno[:, wi * H:(wi + 1) * H],
                                           rno[:, wi * H:(wi + 1) * H])
                        I(ins, sDV, 1)
                        W(v, "vector", sDV, sDV.v)
                        for h in range(H):
                            ins = v.tensor_scalar(
                                out=sqs[:, h * HD:(h + 1) * HD],
                                in0=sqs[:, h * HD:(h + 1) * HD],
                                scalar1=rno[:, wi * H + h: wi * H + h + 1],
                                scalar2=None, op0=ALU.mult)
                            I(ins, sDV, 1)
                        st["nrm_done"] = sDV.v
                    sec("vector", _l2d)
                _th(f_ev)

                for h0 in range(0, H, 2):
                    def f_tq(wi=wi, h0=h0):
                        for h in (h0, h0 + 1):
                            bank = h % 2

                            def _tq(t, h=h, bank=bank):
                                W(t, "tensor", sDV, max(st["nrm_done"],
                                                        st["pt_free"][bank]))
                                ins = TP(t, ptr[bank][0:HD, 0:WIN],
                                         sqs[:, h * HD:(h + 1) * HD])
                                I(ins, sPE, 1)
                                st["tq_done"] = sPE.v
                            sec("tensor", _tq)

                            def _eq(v, h=h, bank=bank):
                                W(v, "vector", sPE, st["tq_done"])
                                col = (wi * H + h) * WIN
                                ins = v.tensor_copy(qkvT[:, col:col + WIN],
                                                    ptr[bank][0:HD, 0:WIN])
                                I(ins, sDV, 1)
                                st["pt_free"][bank] = sDV.v
                            sec("vector", _eq)
                    _th(f_tq)
            def f_qr():
                st["qkvT_rdy"] = sDV.v
            _th(f_qr)

            def mk_uk(h):
                ub = u2k if h % 2 == 0 else u2k2
                def f():
                    def _uk(v):
                        W(v, "vector", sDV, max(st["qkvT_rdy"],
                                                st["u2k_free2"][h % 2]))
                        kcol = (1 * H + h) * WIN
                        pcol = (2 * H + h) * WIN
                        kap = qkvT[:, kcol:kcol + WIN].unsqueeze(2) \
                            .broadcast_to([64, WIN, WIN])
                        pap = qkvT[:, pcol:pcol + WIN].unsqueeze(1) \
                            .broadcast_to([64, WIN, WIN])
                        u3 = ub[:, :].rearrange("p (j k) -> p j k", j=WIN)
                        ins = v.tensor_tensor(u3, kap, pap, ALU.mult)
                        I(ins, sDV, 1)
                        st["uk_done%d" % (h % 2)] = sDV.v
                    sec("vector", _uk)
                return f

            _th(mk_uk(0))
            _th(mk_uk(1))
            for h in range(H):
                def f_att(h=h):
                    ub = u2k if h % 2 == 0 else u2k2

                    def _am(t):
                        W(t, "tensor", sDV, max(st["uk_done%d" % (h % 2)],
                                                st["pt_free"][0],
                                                st["pt_free"][1], st["a_free"]))
                        W(t, "tensor", sAC, max(st["pt_free_a"][0],
                                                st["pt_free_a"][1]))
                        qcol = h * WIN
                        t.matmul(pta[0:WIN, 0:SUB], qkvT[:, qcol:qcol + WIN],
                                 ub[:, 0:SUB], start=True, stop=True)
                        ins = t.matmul(ptb[0:WIN, 0:SUB],
                                       qkvT[:, qcol:qcol + WIN],
                                       ub[:, SUB:D], start=True, stop=True)
                        I(ins, sPE, 1)
                        st["am_done"] = sPE.v
                    sec("tensor", _am)

                    def _aev(v):
                        W(v, "vector", sPE, st["am_done"])
                        W(v, "vector", sGP, st_gp_done)
                        ins = v.tensor_tensor(a_sb[:, 0:SUB], pta[0:WIN, 0:SUB],
                                              mask[:, 0:SUB], ALU.add)
                        I(ins, sDV, 1)
                        ins = v.tensor_tensor(a_sb[:, SUB:D], ptb[0:WIN, 0:SUB],
                                              mask[:, SUB:D], ALU.add)
                        I(ins, sDV, 1)
                        st["pt_free"][0] = sDV.v
                        st["pt_free"][1] = sDV.v
                        st["u2k_free2"][h % 2] = sDV.v
                        W(v, "vector", sDV, sDV.v)
                        ins = v.reduce_max(wst[:, 1:2], a_sb[:, :], axis=AX.X)
                        I(ins, sDV, 1)
                        W(v, "vector", sDV, sDV.v)
                        ins = v.tensor_scalar(out=wst[:, 2:3], in0=wst[:, 1:2],
                                              scalar1=-0.125, scalar2=None,
                                              op0=ALU.mult)
                        I(ins, sDV, 1)
                        st["amask_done"] = sDV.v
                    sec("vector", _aev)

                    def _aex(s):
                        W(s, "scalar", sDV, st["amask_done"])
                        ins = s.activation(a_sb[:, :], a_sb[:, :], AF.Exp,
                                           bias=wst[:, 2:3], scale=0.125,
                                           accum_out=wst[:, 3:4])
                        I(ins, sAC, 1)
                        st["aexp_done"] = sAC.v
                    sec("scalar", _aex)

                    def _anr(v):
                        W(v, "vector", sAC, st["aexp_done"])
                        ins = v.reciprocal(wst[:, 4:5], wst[:, 3:4])
                        I(ins, sDV, 1)
                        W(v, "vector", sDV, sDV.v)
                        ins = v.tensor_scalar(out=a_sb[:, :], in0=a_sb[:, :],
                                              scalar1=wst[:, 4:5], scalar2=None,
                                              op0=ALU.mult)
                        I(ins, sDV, 1)
                        st["p_done"] = sDV.v
                    sec("vector", _anr)
                _th(f_att)
                if h + 2 < H:
                    _th(mk_uk(h + 2))

                for tau0 in range(0, 8, 2):
                    def f_tp(h=h, tau0=tau0):
                        for tau in (tau0, tau0 + 1):
                            bank = tau % 2

                            def _tp(t, tau=tau, bank=bank):
                                W(t, "tensor", sDV, max(st["p_done"],
                                                        st["pt_free"][bank]))
                                W(t, "tensor", sAC, st["pt_free_a"][bank])
                                ins = TP(t, ptr[bank][0:128, 0:WIN],
                                         a_sb[:, tau * 128:(tau + 1) * 128])
                                I(ins, sPE, 1)
                                st["tp_done"] = sPE.v
                            sec("tensor", _tp)

                            def _te(v, tau=tau, bank=bank):
                                W(v, "vector", sPE, st["tp_done"])
                                ins = v.tensor_copy(
                                    pt_sb[:, tau * WIN:(tau + 1) * WIN],
                                    ptr[bank][0:128, 0:WIN])
                                I(ins, sDV, 1)
                                st["pt_free"][bank] = sDV.v
                                if tau == 7:
                                    st["a_free"] = sDV.v
                            sec("vector", _te)
                    _th(f_tp)

                def f_v2(h=h):
                    def _v2(t):
                        W(t, "tensor", sDV, max(st["vtm_rdy"],
                                                st["pt_free"][0]))
                        W(t, "tensor", sAC, st["pt_free_a"][0])
                        ins = t.matmul(pta[:, 0:HD], bk[:, :],
                                       v_tm[:, h * HD:(h + 1) * HD],
                                       start=True, stop=True)
                        I(ins, sPE, 1)
                        st["v2_done"] = sPE.v
                    sec("tensor", _v2)

                    def _v2e(v):
                        W(v, "vector", sPE, st["v2_done"])
                        W(v, "vector", sDV, sDV.v)
                        ins = v.tensor_copy(vrp2[:, :], pta[:, 0:HD])
                        I(ins, sDV, 1)
                        st["pt_free"][0] = sDV.v
                        st["v2e_done"] = sDV.v
                    sec("vector", _v2e)
                _th(f_v2)

                for tau in range(8):
                    def f_uv(h=h, tau=tau):
                        bank = tau % 2

                        def _u1(t):
                            W(t, "tensor", sDV, max(st["v2e_done"],
                                                    st["pt_free"][bank]))
                            W(t, "tensor", sAC, st["pt_free_a"][bank])
                            ins = t.matmul(ptr[bank][:, 0:HD],
                                           bj[:, tau * 128:(tau + 1) * 128],
                                           v_tm[:, h * HD:(h + 1) * HD],
                                           start=True, stop=True)
                            I(ins, sPE, 1)
                            st["u1_done"] = sPE.v
                        sec("tensor", _u1)

                        def _u4(v):
                            W(v, "vector", sPE, st["u1_done"])
                            ins = v.tensor_tensor(uvt[:, tau * HD:(tau + 1) * HD],
                                                  vrp2[:, :], ptr[bank][:, 0:HD],
                                                  ALU.mult)
                            I(ins, sDV, 1)
                            st["pt_free"][bank] = sDV.v
                            if tau == 7:
                                st["uv_done"] = sDV.v
                        sec("vector", _u4)
                    _th(f_uv)

                def f_z(h=h):
                    oc = h // 2
                    base = (h % 2) * HD

                    def _zm(t):
                        W(t, "tensor", sDV, max(st["uv_done"], st["pzb_free"]))
                        ins = None
                        for tau in range(8):
                            ins = t.matmul(pzb[base:base + HD, 0:WIN],
                                           uvt[:, tau * HD:(tau + 1) * HD],
                                           pt_sb[:, tau * WIN:(tau + 1) * WIN],
                                           start=(tau == 0), stop=(tau == 7),
                                           tile_position=(0, base) if base else None)
                        I(ins, sPE, 1)
                        st["zm_done"] = sPE.v
                    sec("tensor", _zm)

                    def _ze(v):
                        W(v, "vector", sPE, st["zm_done"])
                        ins = v.tensor_copy(
                            zt[base:base + HD, oc * WIN:(oc + 1) * WIN],
                            pzb[base:base + HD, 0:WIN])
                        I(ins, sDV, 1)
                        st["pzb_free"] = sDV.v
                    sec("vector", _ze)
                _th(f_z)

            def wo_dma_th(kc):
                def f():
                    slot = kc % 4

                    def _od(g):
                        fr = st["aw_free"][slot]
                        if fr is not None:
                            W(g, "gpsimd", fr[0], fr[1])
                        ins = g.dma_start(out=awr[slot][:, :],
                                          in_=wo_d[kc * 128:(kc + 1) * 128, :])
                        I(ins, sAW[slot], 16)
                        st["wor%d" % kc] = sAW[slot].v
                    sec("gpsimd", _od)
                return f

            for k0 in range(4):
                _th(wo_dma_th(k0))
            for kc in range(DC):
                def f_om(kc=kc):
                    slot = kc % 4

                    def _om(t):
                        if kc == 0:
                            W(t, "tensor", sDV, max(st["pzb_free"],
                                                    st["pt_free"][0],
                                                    st["pt_free"][1]))
                            W(t, "tensor", sAC, max(st["pt_free_a"][0],
                                                    st["pt_free_a"][1]))
                        W(t, "tensor", sAW[slot], st["wor%d" % kc])
                        t.matmul(pta[0:WIN, 0:SUB], zt[:, kc * WIN:(kc + 1) * WIN],
                                 awr[slot][:, 0:SUB],
                                 start=(kc == 0), stop=(kc == DC - 1))
                        ins = t.matmul(ptb[0:WIN, 0:SUB],
                                       zt[:, kc * WIN:(kc + 1) * WIN],
                                       awr[slot][:, SUB:D],
                                       start=(kc == 0), stop=(kc == DC - 1))
                        I(ins, sPE, 1)
                        st["aw_free"][slot] = (sPE, sPE.v)
                        st["zo_done"] = sPE.v
                    sec("tensor", _om)
                    if kc + 4 < DC:
                        wo_dma_th(kc + 4)()
                _th(f_om)

            def f_zoe():
                def _zoe(v):
                    W(v, "vector", sPE, st["zo_done"])
                    ins = v.tensor_copy(ao_tm[:, 0:SUB], pta[0:WIN, 0:SUB])
                    I(ins, sDV, 1)
                    ins = v.tensor_copy(ao_tm[:, SUB:D], ptb[0:WIN, 0:SUB])
                    I(ins, sDV, 1)
                    st["pt_free"][0] = sDV.v
                    st["pt_free"][1] = sDV.v
                    st["zoe_done"] = sDV.v
                sec("vector", _zoe)
            _th(f_zoe)

            for dc0 in range(0, DC, 2):
                def f_rx(dc0=dc0):
                    for dc in (dc0, dc0 + 1):
                        bank = dc % 2

                        def _rx(t, dc=dc, bank=bank):
                            W(t, "tensor", sDV, max(st["zoe_done"],
                                                    st["pt_free"][bank]))
                            W(t, "tensor", sAC, st["pt_free_a"][bank])
                            ins = TP(t, ptr[bank][0:WIN, 0:128],
                                     xw32[:, dc * WIN:(dc + 1) * WIN])
                            I(ins, sPE, 1)
                            st["rx_done"] = sPE.v
                        sec("tensor", _rx)

                        def _ra(v, dc=dc, bank=bank):
                            W(v, "vector", sPE, st["rx_done"])
                            ins = v.tensor_tensor(
                                ao_tm[:, dc * 128:(dc + 1) * 128],
                                ao_tm[:, dc * 128:(dc + 1) * 128],
                                ptr[bank][0:WIN, 0:128], ALU.add)
                            I(ins, sDV, 1)
                            st["pt_free"][bank] = sDV.v
                            st["ao_done"] = sDV.v
                        sec("vector", _ra)
                _th(f_rx)

            def f_ln():
                def _w0(v):
                    W(v, "vector", sDV, st["ao_done"])
                    ins = v.reduce_sum(wst[:, 0:1], ao_tm[:, :], axis=AX.X)
                    I(ins, sDV, 1)
                    st["w0_done"] = sDV.v
                sec("vector", _w0)

                def _w1a(s):
                    W(s, "scalar", sDV, st["w0_done"])
                    ins = s.activation(sqs[:, :], ao_tm[:, :], AF.Square,
                                       accum_out=wst[:, 1:2])
                    I(ins, sAC, 1)
                    st["w1_done"] = sAC.v
                sec("scalar", _w1a)

                def _w2a(v):
                    W(v, "vector", sAC, st["w1_done"])
                    ins = v.tensor_scalar(out=wst[:, 2:3], in0=wst[:, 0:1],
                                          scalar1=1.0 / D, scalar2=None,
                                          op0=ALU.mult)
                    I(ins, sDV, 1)
                    ins = v.tensor_scalar(out=wst[:, 3:4], in0=wst[:, 1:2],
                                          scalar1=1.0 / D, scalar2=None,
                                          op0=ALU.mult)
                    I(ins, sDV, 1)
                    W(v, "vector", sDV, sDV.v)
                    ins = v.tensor_tensor(wst[:, 4:5], wst[:, 2:3], wst[:, 2:3],
                                          ALU.mult)
                    I(ins, sDV, 1)
                    W(v, "vector", sDV, sDV.v)
                    ins = v.tensor_tensor(wst[:, 4:5], wst[:, 3:4], wst[:, 4:5],
                                          ALU.subtract)
                    I(ins, sDV, 1)
                    st["w2_done"] = sDV.v
                sec("vector", _w2a)

                def _w3(s):
                    W(s, "scalar", sDV, st["w2_done"])
                    ins = s.activation(wst[:, 5:6], wst[:, 4:5], AF.Ln, bias=1e-5)
                    I(ins, sAC, 1)
                    W(s, "scalar", sAC, sAC.v)
                    ins = s.activation(wst[:, 6:7], wst[:, 5:6], AF.Exp,
                                       scale=-0.5)
                    I(ins, sAC, 1)
                    st["w3_done"] = sAC.v
                sec("scalar", _w3)

                def _w4(v):
                    W(v, "vector", sAC, st["w3_done"])
                    ins = v.tensor_tensor(wst[:, 7:8], wst[:, 2:3], wst[:, 6:7],
                                          ALU.mult)
                    I(ins, sDV, 1)
                    W(v, "vector", sDV, sDV.v)
                    ins = v.tensor_scalar(out=ao_tm[:, :], in0=ao_tm[:, :],
                                          scalar1=wst[:, 6:7],
                                          scalar2=wst[:, 7:8],
                                          op0=ALU.mult, op1=ALU.subtract)
                    I(ins, sDV, 1)
                    st["sw_done"] = sDV.v
                sec("vector", _w4)
            _th(f_ln)

            for dc0 in range(0, DC, 2):
                def f_pat(dc0=dc0):
                    for dc in (dc0, dc0 + 1):
                        bank = dc % 2

                        def _p1(t, dc=dc, bank=bank):
                            W(t, "tensor", sDV, max(st["sw_done"],
                                                    st["pt_free"][bank]))
                            W(t, "tensor", sAC, st["pt_free_a"][bank])
                            ins = TP(t, ptr[bank][0:128, 0:WIN],
                                     ao_tm[:, dc * 128:(dc + 1) * 128])
                            I(ins, sPE, 1)
                            st["p1_done"] = sPE.v
                        sec("tensor", _p1)

                        def _p2(v, dc=dc, bank=bank):
                            W(v, "vector", sPE, st["p1_done"])
                            col = dc * MAC + (MAC - WIN)
                            ins = v.tensor_scalar(
                                out=s_all[:, col:col + WIN],
                                in0=ptr[bank][0:128, 0:WIN],
                                scalar1=gb4[:, 2 * DC + dc:2 * DC + dc + 1],
                                scalar2=gb4[:, 3 * DC + dc:3 * DC + dc + 1],
                                op0=ALU.mult, op1=ALU.add)
                            I(ins, sDV, 1)
                            st["pt_free"][bank] = sDV.v
                        sec("vector", _p2)
                    if dc0 == DC - 2:
                        st["s_patch"] = sDV.v
                        st["attn_ring_free"] = sDV.v
                        for _i in range(4):
                            st["ring_free"][_i] = (sDV, sDV.v)
                _th(f_pat)

        def drain_attn(n):
            while n > 0 and attn_thunks:
                attn_thunks.pop(0)()
                n -= 1

        epi_thunks = []

        def drain_epi(n):
            while n > 0 and epi_thunks:
                epi_thunks.pop(0)()
                n -= 1

        def epilogue_mk(mu, mi):
            tok0 = mu * MAC
            def _xcp(sy, mu=mu, tok0=tok0):
                ins = sy.dma_start(out=out_d[tok0: tok0 + MAC, :],
                                   in_=x_d[tok0: tok0 + MAC, :])
                I(ins, sCPY, 16)
                for c in range(NCH):
                    st["xcopy%d" % c] = sCPY.v
            sec("sync", _xcp)
            for c in range(NCH):
                def chunk_th(c=c, mu=mu, mi=mi, tok0=tok0):
                    sl = c % 2
                    emit_chunk(mu, mi, tok0, c, sl)
                epi_thunks.append(chunk_th)

        def emit_chunk(mu, mi, tok0, c, sl):
                for dcg in range(2):
                    ti = dcg % 2

                    def _si(s, mu=mu, c=c, dcg=dcg, ti=ti):
                        W(s, "scalar", sDV,
                          st["mixfin_s%d_g%d" % (0 if c < 4 else 1, dcg)])
                        W(s, "scalar", sDV, st["gt_free"][ti])
                        msl = mix[:, :].rearrange("p (g t) -> p g t", g=DC)[
                            :, dcg * 4:(dcg + 1) * 4, c * 128:(c + 1) * 128]
                        ins = s.activation(gt[:, ti * SUB:(ti + 1) * SUB]
                                           .rearrange("p (g t) -> p g t", g=4),
                                           msl, AF.Silu)
                        I(ins, sAC, 1)
                        st["epi_silu_done"] = sAC.v
                    sec("scalar", _si)
                    sil = sAC.v

                    def _de(v, mu=mu, c=c, dcg=dcg, ti=ti, sil=sil):
                        W(v, "vector", sAC, sil)
                        if ATTN and mu == 1:
                            W(v, "vector", sDV, st["s_patch"])
                        ssl = s_all[:, :].rearrange("p (g t) -> p g t", g=DC)[
                            :, dcg * 4:(dcg + 1) * 4, c * 128:(c + 1) * 128]
                        ins = v.tensor_tensor(dt[:, ti * SUB:(ti + 1) * SUB]
                                              .rearrange("p (g t) -> p g t", g=4),
                                              gt[:, ti * SUB:(ti + 1) * SUB]
                                              .rearrange("p (g t) -> p g t", g=4),
                                              ssl, ALU.mult)
                        I(ins, sDV, 1)
                        st["gt_free"][ti] = sDV.v
                    sec("vector", _de)
                    ded = sDV.v

                    for q in range(4):
                        bank = q % 2

                        def _dt(t, ti=ti, q=q, bank=bank, ded=ded):
                            W(t, "tensor", sDV, max(ded, st["pt_free"][bank]))
                            W(t, "tensor", sAC, st["pt_free_a"][bank])
                            ins = TP(t, ptr[bank][:, 0:128],
                                     dt[:, ti * SUB + q * 128:
                                        ti * SUB + (q + 1) * 128])
                            I(ins, sPE, 1)
                        sec("tensor", _dt)
                        dtd = sPE.v

                        def _oa(v, c=c, dcg=dcg, sl=sl, ti=ti, q=q, bank=bank,
                                dtd=dtd):
                            W(v, "vector", sPE, dtd)
                            W(v, "vector", sOS[ti], st["ost_free"][ti])
                            ins = v.tensor_copy(
                                ost[:, ti * SUB + q * 128: ti * SUB + (q + 1) * 128],
                                ptr[bank][:, 0:128])
                            I(ins, sDV, 1)
                            st["pt_free"][bank] = sDV.v
                        sec("vector", _oa)
                    oad = sDV.v

                    def _os(g, mu=mu, c=c, dcg=dcg, ti=ti, oad=oad):
                        W(g, "gpsimd", sDV, oad)
                        W(g, "gpsimd", sCPY, st["xcopy%d" % c])
                        ins = g.dma_start(
                            out=out_d[tok0 + c * 128: tok0 + (c + 1) * 128,
                                      dcg * SUB:(dcg + 1) * SUB],
                            in_=ost[:, ti * SUB:(ti + 1) * SUB],
                            accum_op=ALU.add)
                        I(ins, sOS[ti], 16)
                        st["ost_free"][ti] = sOS[ti].v
                    sec("gpsimd", _os)

        phase1(MACRO_ORDER[0], 0)
        for mi, mu in enumerate(MACRO_ORDER):
            if ATTN and mu == 1:
                build_attention()
                drain_attn(4)
            epilogue_mk(mu, mi)
            if mi + 1 < len(MACRO_ORDER):
                phase1(MACRO_ORDER[mi + 1], mi + 1, defer_head=True)
            expert_loop(mu, mi)
            drain_epi(len(epi_thunks))

        def _fin(sy):
            for sem in (sOS[0], sOS[1], sXC[0], sXC[1], sCPY, *sAW,
                        sCX, sB2, sW1, sW2):
                W(sy, "sync", sem, sem.v)
        sec("sync", _fin)

    es.close()
    _bass_rust.insert_act_table_loads(nc, _act_tables() if _USE_ACT_FILTER
                                      else None)
    return nc


_USE_ACT_FILTER = True


def _act_tables():
    try:
        import json
        from pathlib import Path
        import neuronxcc
        root = Path(neuronxcc.__file__).parent / "pwp"
        cand = sorted(root.glob("pwp_bin_*/act_info.json"))
        info = json.loads(cand[0].read_bytes())
        tables = {}
        for ent in info["act_func_sets"]:
            funcs = set(ent["act"].keys())
            if ent["name"] == "exp_and_others":
                funcs.discard("exp")
            if ent["name"] == "natural_log":
                funcs.discard("ln")
            tables[ent["name"]] = {
                mybir.ActivationFunctionType.from_pwp(v) for v in funcs}
        return tables
    except Exception:
        return None


_NC_CACHE = None


def _get_nc():
    global _NC_CACHE
    if _NC_CACHE is None:
        _NC_CACHE = build_program()
    return _NC_CACHE


def kernel(**inputs):
    nc = _get_nc()
    x = np.ascontiguousarray(np.asarray(inputs["L0_attn_x"], dtype=np.float32))
    common = {
        "wq": np.ascontiguousarray(np.asarray(inputs["L0_attn_W_Q"], np.float32)),
        "wk": np.ascontiguousarray(np.asarray(inputs["L0_attn_W_K"], np.float32)),
        "wkp": np.ascontiguousarray(np.asarray(inputs["L0_attn_W_Kp"], np.float32)),
        "wv": np.ascontiguousarray(np.asarray(inputs["L0_attn_W_V"], np.float32)),
        "wo": np.ascontiguousarray(np.asarray(inputs["L0_attn_W_O"], np.float32)),
        "gam": np.ascontiguousarray(np.asarray(inputs["L0_attn_ln_gamma"], np.float32)),
        "bet": np.ascontiguousarray(np.asarray(inputs["L0_attn_ln_beta"], np.float32)),
        "lng": np.ascontiguousarray(np.asarray(inputs["L0_ffn_ln_g"], np.float32)),
        "lnb": np.ascontiguousarray(np.asarray(inputs["L0_ffn_ln_b"], np.float32)),
        "rw": np.ascontiguousarray(np.asarray(inputs["L0_ffn_rW"], np.float32)),
        "rb": np.ascontiguousarray(np.asarray(inputs["L0_ffn_rb"], np.float32)),
        "w1": np.ascontiguousarray(np.asarray(inputs["L0_ffn_W1"], np.float32)),
        "b1": np.ascontiguousarray(np.asarray(inputs["L0_ffn_b1"], np.float32)),
        "w2": np.ascontiguousarray(np.asarray(inputs["L0_ffn_W2"], np.float32)),
        "b2": np.ascontiguousarray(np.asarray(inputs["L0_ffn_b2"], np.float32)),
    }
    in_maps = [dict(common, x=np.ascontiguousarray(x[i])) for i in range(8)]
    res = run_bass_kernel_spmd(nc, in_maps, core_ids=list(range(8)))
    return np.stack([np.asarray(r["out"], np.float32) for r in res.results], axis=0)
